# version 01
# baseline (speedup 1.0000x reference)
import contextlib
import numpy as np
import ml_dtypes
import concourse.bass as bass
import concourse.mybir as mybir
from concourse.bass import ds
from concourse.bass_utils import run_bass_kernel_spmd

F32 = mybir.dt.float32
BF16 = mybir.dt.bfloat16
AF = mybir.ActivationFunctionType
ALU = mybir.AluOpType
AX = mybir.AxisListType

T, C, KC, TC = 2048, 1024, 8, 16
HP, DH, QW = 8, 64, 512
E, DFF, FG, G, FCG, NTT, TTW = 8, 4096, 512, 8, 4, 8, 256

ST_A, ST_B = 0, 512
AV_STD, AV_P = 1024, 2048
BCAST = 3072
HPS = 2048

SEED = {"sAV": 2, "sAVE": 2, "sSl": 2, "sHw": 2, "sW2": 1}


def build(nc, stage=99, taps=()):
    x_d = nc.dram_tensor("x", [T, C], F32, kind="ExternalInput").ap()
    wqT = nc.dram_tensor("wqT", [C, C], F32, kind="ExternalInput").ap()
    wkT = nc.dram_tensor("wkT", [C, C], F32, kind="ExternalInput").ap()
    wkpT = nc.dram_tensor("wkpT", [C, C], F32, kind="ExternalInput").ap()
    wvT = nc.dram_tensor("wvT", [C, C], F32, kind="ExternalInput").ap()
    woT = nc.dram_tensor("woT", [C, C], F32, kind="ExternalInput").ap()
    alpha_d = nc.dram_tensor("alpha", [1, 1], F32, kind="ExternalInput").ap()
    wg_d = nc.dram_tensor("wg", [C, E], F32, kind="ExternalInput").ap()
    w1_d = nc.dram_tensor("w1", [E, C, DFF], F32, kind="ExternalInput").ap()
    w2_d = nc.dram_tensor("w2", [E, DFF, C], F32, kind="ExternalInput").ap()
    w3_d = nc.dram_tensor("w3", [E, C, DFF], F32, kind="ExternalInput").ap()
    y_d = nc.dram_tensor("y", [T, C], F32, kind="ExternalOutput").ap()
    tap_d = {}
    for tname, tshape, tdt in taps:
        tap_d[tname] = nc.dram_tensor("tap_" + tname, list(tshape), tdt,
                                      kind="ExternalOutput").ap()

    id_f32 = nc.inline_tensor(np.eye(128, dtype=np.float32), "c_idf").ap()
    ones_fc = nc.inline_tensor(np.ones((128, 128), np.float32), "c_onesf").ap()
    ones_f1 = nc.inline_tensor(np.ones((128, 1), np.float32), "c_onesf1").ap()
    sel_np = np.zeros((8, 8 * 128), np.float32)
    for e in range(8):
        sel_np[e, e * 128:(e + 1) * 128] = 1.0
    sel_c = nc.inline_tensor(sel_np, "c_sel8").ap()
    iota_np = np.tile(np.arange(8, dtype=np.float32)[None, :], (128, 1))
    iota_c = nc.inline_tensor(iota_np, "c_iota8").ap()
    iotp_c = nc.inline_tensor(iota_np + 1000.0, "c_iotaP").ap()

    ctx = contextlib.ExitStack()
    with ctx:
        xf32 = ctx.enter_context(nc.sbuf_tensor("xf32", [128, KC * T], F32))
        xbf = ctx.enter_context(nc.sbuf_tensor("xbf", [128, KC * T], BF16))
        ytm = ctx.enter_context(nc.sbuf_tensor("ytm", [128, 2 * 1024], F32))
        ysq = ctx.enter_context(nc.sbuf_tensor("ysq", [128, 2 * 1024], F32))
        idf = ctx.enter_context(nc.sbuf_tensor("idf", [128, 128], F32))
        idb = ctx.enter_context(nc.sbuf_tensor("idb", [128, 128], BF16))
        onesf = ctx.enter_context(nc.sbuf_tensor("onesf", [128, 128], F32))
        onesb = ctx.enter_context(nc.sbuf_tensor("onesb", [128, 1], BF16))
        ones64 = ctx.enter_context(nc.sbuf_tensor("ones64", [128, 64], BF16))
        selb = ctx.enter_context(nc.sbuf_tensor("selb", [8, 8 * 128], F32))
        iota8 = ctx.enter_context(nc.sbuf_tensor("iota8", [128, 8], F32))
        iotaP = ctx.enter_context(nc.sbuf_tensor("iotaP", [128, 8], F32))
        alpha_a = ctx.enter_context(nc.sbuf_tensor("alpha_a", [128, 2], F32))
        stat = ctx.enter_context(nc.sbuf_tensor("stat", [128, 10], F32))
        psum = ctx.enter_context(nc.psum_tensor("arena", [128, 4096], F32))

        sem_names = [
            "ldx0", "ldx1", "cl", "clg", "sMS", "sXT", "sXE", "sALP", "sALPE",
            "watq", "watk", "watkp", "watv", "wol0", "wol1", "wgl", "sPJ", "sPE", "sPV",
            "sST", "sEX", "sAV", "sAVE", "sRC", "sBC", "sBL", "sVM",
            "sWO", "sWOE",
            "sL1T", "sL1C", "sLNa", "sLNb", "sL1N", "sL1B", "sL1E",
            "sRT", "sRTE", "sREX", "sRTT", "sRTC",
            "sWB", "sWBE", "sWC", "wf0", "wf1", "sA", "sB", "sSl", "sHw", "sW2", "sEv",
            "sFT", "sFC", "sFN", "st", "sty0", "sty1",
        ]
        sm = {n: ctx.enter_context(nc.semaphore(n)) for n in sem_names}
        blk = ctx.enter_context(nc.Block())

        n_stores = [0]

        def tap(sync, name, src_ap, wait_sem, wait_val):
            if name in tap_d:
                sync.wait_ge(sm[wait_sem], wait_val)
                sync.dma_start(out=tap_d[name], in_=src_ap).then_inc(sm["st"], 16)
                n_stores[0] += 1

        @blk.sync
        def _(sync):
            for dst, src in ((idf[:], id_f32), (onesf[:], ones_fc),
                             (selb[:], sel_c), (iota8[:], iota_c),
                             (iotaP[:], iotp_c), (alpha_a[0:1, 0:1], alpha_d)):
                sync.dma_start(out=dst, in_=src).then_inc(sm["cl"], 16)
            for tc in range(TC):
                if tc >= 2:
                    sync.wait_ge(sm["sXT"], tc - 1)
                sync.dma_start(out=ytm[:, (tc % 2) * 1024:(tc % 2) * 1024 + 1024],
                               in_=x_d[tc * 128:(tc + 1) * 128, :]).then_inc(sm["ldx%d" % (tc % 2)], 16)

        @blk.gpsimd
        def _(g):
            for s, v in SEED.items():
                g.sem_inc(sm[s], v)
            g.dma_start(out=idb[:], in_=id_f32).then_inc(sm["clg"], 16)
            g.dma_start(out=onesb[:], in_=ones_f1).then_inc(sm["clg"], 16)
            g.dma_start(out=ones64[:], in_=ones_fc[:, 0:64] if hasattr(ones_fc, '__getitem__') else ones_fc).then_inc(sm["clg"], 16)

        @blk.tensor
        def _(tensor):
            tensor.wait_ge(sm["cl"], 16 * 6)
            tensor.wait_ge(sm["clg"], 16 * 3)
            tensor.wait_ge(sm["sMS"], 1)
            tensor.matmul(psum[:, 2048:2049], onesf[0:1, 0:128], alpha_a[0:1, 0:1],
                          start=True, stop=True, skip_group_check=True).then_inc(sm["sALP"], 1)
            for tc in range(TC):
                tensor.wait_ge(sm["ldx%d" % (tc % 2)], 16 * (tc // 2 + 1))
                if tc >= 2:
                    tensor.wait_ge(sm["sXE"], tc - 1)
                pc = (tc % 2) * 1024
                for cc in range(KC):
                    ins = tensor.transpose(psum[:, pc + cc * 128:pc + cc * 128 + 128],
                                           ytm[:, (tc % 2) * 1024 + cc * 128:(tc % 2) * 1024 + cc * 128 + 128],
                                           idf[:])
                ins.then_inc(sm["sXT"], 1)

        @blk.vector
        def _(vector):
            vector.memset(psum[:, :], 0.0).then_inc(sm["sMS"], 1)
            vector.wait_ge(sm["sALP"], 1)
            vector.tensor_copy(alpha_a[:, 0:1], psum[:, 2048:2049])
            vector.drain()
            vector.tensor_scalar(alpha_a[:, 1:2], alpha_a[:, 0:1], -1.0, 1.0,
                                 ALU.mult, ALU.add).then_inc(sm["sALPE"], 1)
            vector.drain()
            for tc in range(TC):
                vector.wait_ge(sm["sXT"], tc + 1)
                pc = (tc % 2) * 1024
                pin = psum[:, pc:pc + 1024].rearrange("p (cc t) -> p cc t", cc=KC)
                vector.tensor_copy(
                    xf32[:, :].rearrange("p (cc t) -> p cc t", cc=KC)[:, :, tc * 128:tc * 128 + 128],
                    pin)
                vector.tensor_copy(
                    xbf[:, :].rearrange("p (cc t) -> p cc t", cc=KC)[:, :, tc * 128:tc * 128 + 128],
                    pin).then_inc(sm["sXE"], 1)

        actx = contextlib.ExitStack()
        with actx:
            wq_p = actx.enter_context(nc.sbuf_tensor("wq_p", [128, 2 * 1024], BF16))
            wk_p = actx.enter_context(nc.sbuf_tensor("wk_p", [128, 2 * 1024], BF16))
            wkp_p = actx.enter_context(nc.sbuf_tensor("wkp_p", [128, 2 * 1024], BF16))
            wv_p = actx.enter_context(nc.sbuf_tensor("wv_p", [128, 2 * 1024], BF16))
            q_fm = actx.enter_context(nc.sbuf_tensor("q_fm", [128, T], BF16))
            k_fm = actx.enter_context(nc.sbuf_tensor("k_fm", [128, T], BF16))
            kp_fm = actx.enter_context(nc.sbuf_tensor("kp_fm", [128, T], BF16))
            v_p = actx.enter_context(nc.sbuf_tensor("v_p", [128, 16 * 130], BF16))
            attnT = actx.enter_context(nc.sbuf_tensor("attnT", [128, 2 * 1024], BF16))
            av_sb = actx.enter_context(nc.sbuf_tensor("av_sb", [128, 2048], F32))
            rcp_sb = actx.enter_context(nc.sbuf_tensor("rcp_sb", [128, 1024], F32))
            rb = actx.enter_context(nc.sbuf_tensor("rb", [128, 1024], F32))
            wo_p = actx.enter_context(nc.sbuf_tensor("wo_p", [128, 2 * 1024], BF16))
            attn_out = actx.enter_context(nc.sbuf_tensor("attn_out", [128, HP * T], BF16))

            if stage >= 1:
                @blk.gpsimd
                def _(g):
                    for hp in range(HP):
                        par = hp % 2
                        kinds = (("watq", wq_p, wqT, 1), ("watk", wk_p, wkT, 6),
                                 ("watkp", wkp_p, wkpT, 11), ("watv", wv_p, wvT, 19))
                        for wsem, wten, wsrc, gate in kinds:
                            if hp >= 1:
                                g.wait_ge(sm["sPJ"], 19 * (hp - 1) + gate)
                            g.dma_start(
                                out=wten[:, par * 1024:(par + 1) * 1024].rearrange(
                                    "p (kc c) -> p kc c", kc=KC),
                                in_=wsrc.rearrange("(kc p) c -> p kc c", p=128)[:, :, hp * 128:(hp + 1) * 128],
                            ).then_inc(sm[wsem], 16)
                    if stage >= 2:
                        for mc in range(KC):
                            if mc >= 2:
                                g.wait_ge(sm["sWO"], mc - 1)
                            g.dma_start(
                                out=wo_p[:, (mc % 2) * 1024:(mc % 2) * 1024 + 1024].rearrange(
                                    "p (kc c) -> p kc c", kc=KC),
                                in_=woT.rearrange("(kc p) c -> p kc c", p=128)[:, :, mc * 128:(mc + 1) * 128],
                            ).then_inc(sm["wol%d" % (mc % 2)], 16)

                @blk.tensor
                def _(tensor):
                    tensor.wait_ge(sm["sXE"], TC)
                    tensor.wait_ge(sm["sALPE"], 1)
                    rWAT = tensor.alloc_register("rWAT")
                    rPVB = tensor.alloc_register("rPVB")
                    rT1 = tensor.alloc_register("rT1")
                    rPEW = tensor.alloc_register("rPEW")
                    rPEk = tensor.alloc_register("rPEk")
                    rKCB = tensor.alloc_register("rKCB")
                    rEX = tensor.alloc_register("rEX")
                    rAVE = tensor.alloc_register("rAVE")
                    rRC = tensor.alloc_register("rRC")
                    rT2 = tensor.alloc_register("rT2")
                    tensor.reg_mov(rWAT, 0)
                    tensor.reg_mov(rPVB, 0)
                    tensor.reg_mov(rPEW, 0)
                    tensor.reg_mov(rKCB, 0)
                    tensor.reg_mov(rAVE, 0)
                    tensor.reg_mov(rRC, 0)
                    with tensor.Fori(0, HP // 2) as hpi:
                        for hpar in range(2):
                            po = hpar * 1024
                            tensor.wait_ge(sm["sEX"], rKCB)
                            tensor.reg_alu(rT2, rAVE, 2, ALU.add)
                            tensor.wait_ge(sm["sAVE"], rT2)
                            tensor.reg_add(rWAT, rWAT, 16)
                            vtile = [0]

                            def vblock(nt, tensor=tensor, po=po, vtile=vtile):
                                for t in range(vtile[0], vtile[0] + nt):
                                    tensor.wait_ge(sm["watv"], rWAT)
                                    if t >= 2:
                                        tensor.reg_alu(rT1, rPVB, t - 1, ALU.add)
                                        tensor.wait_ge(sm["sPV"], rT1)
                                    vp = 2048 + (t % 2) * 512
                                    for kcj in range(KC):
                                        ins = tensor.matmul(
                                            psum[:, vp:vp + 128],
                                            xbf[:, kcj * T + t * 128:kcj * T + t * 128 + 128],
                                            wv_p[:, po + kcj * 128:po + kcj * 128 + 128],
                                            start=(kcj == 0), stop=(kcj == KC - 1),
                                            skip_group_check=True)
                                    ins.then_inc(sm["sPJ"], 1)
                                vtile[0] += nt

                            for wi, (wsn, wt) in enumerate((("watq", wq_p), ("watk", wk_p), ("watkp", wkp_p))):
                                tensor.wait_ge(sm[wsn], rWAT)
                                if wi >= 1:
                                    tensor.reg_add(rPEW, rPEW, 1)
                                    tensor.wait_ge(sm["sPE"], rPEW)
                                for kcj in range(KC):
                                    for nb in range(4):
                                        ins = tensor.matmul(
                                            psum[:, nb * 512:nb * 512 + 512],
                                            wt[:, po + kcj * 128:po + kcj * 128 + 128],
                                            xbf[:, kcj * T + nb * 512:kcj * T + nb * 512 + 512],
                                            start=(kcj == 0), stop=(kcj == KC - 1),
                                            skip_group_check=True)
                                ins.then_inc(sm["sPJ"], 1)
                                vblock(4 if wi < 2 else 8)

                            tensor.reg_add(rPEW, rPEW, 1)
                            tensor.wait_ge(sm["sPE"], rPEW)
                            for qh in range(4):
                                tensor.reg_add(rRC, rRC, 1)
                                if hpar == 0 and qh == 0:
                                    tensor.reg_alu(rT2, hpi, 1, ALU.mult)
                                    guard = tensor.If_ne(rT2, 0)
                                    guard.__enter__()
                                tensor.reg_alu(rT2, rRC, 1, ALU.subtract)
                                tensor.wait_ge(sm["sRC"], rT2)
                                tensor.reg_alu(rT1, rT2, 1, ALU.subtract)
                                tensor.wait_ge(sm["sBL"], rT1)
                                for mi in range(2):
                                    bco = BCAST + mi * 512
                                    ins = tensor.matmul(
                                        psum[0:64, bco:bco + 512],
                                        onesf[64:65, 0:64], rcp_sb[64:65, mi * 512:mi * 512 + 512],
                                        start=True, stop=True, skip_group_check=True)
                                    ins = tensor.matmul(
                                        psum[64:128, bco:bco + 512],
                                        onesf[0:1, 0:64], rcp_sb[0:1, mi * 512:mi * 512 + 512],
                                        start=True, stop=True, skip_group_check=True)
                                ins.then_inc(sm["sBC"], 1)
                                if hpar == 0 and qh == 0:
                                    guard.__exit__(None, None, None)
                                tensor.reg_alu(rEX, rKCB, qh * 32, ALU.add)
                                for mat in range(2):
                                    kf = kp_fm if mat else k_fm
                                    avo = AV_P if mat else AV_STD
                                    tensor.reg_add(rAVE, rAVE, 1)
                                    tensor.wait_ge(sm["sAVE"], rAVE)
                                    tensor.reg_alu(rPEk, rPVB, 1, ALU.add)
                                    if mat == 1:
                                        tensor.reg_alu(rT1, rPVB, 16, ALU.add)
                                        tensor.wait_ge(sm["sPV"], rT1)
                                    for kc in range(16):
                                        tensor.wait_ge(sm["sEX"], rEX)
                                        ins = tensor.matmul(
                                            psum[:, ST_A:ST_A + 512],
                                            kf[0:64, kc * 128:kc * 128 + 128],
                                            q_fm[0:64, qh * QW:qh * QW + QW],
                                            start=True, stop=True, tile_position=(0, 0),
                                            skip_group_check=True)
                                        ins = tensor.matmul(
                                            psum[:, ST_B:ST_B + 512],
                                            kf[64:128, kc * 128:kc * 128 + 128],
                                            q_fm[64:128, qh * QW:qh * QW + QW],
                                            start=True, stop=True, tile_position=(64, 0),
                                            skip_group_check=True)
                                        ins.then_inc(sm["sST"], 1)
                                        tensor.reg_add(rEX, rEX, 1)
                                        tensor.wait_ge(sm["sEX"], rEX)
                                        tensor.wait_ge(sm["sPV"], rPEk)
                                        tensor.reg_add(rPEk, rPEk, 1)
                                        slot = (kc % 2) * 1024
                                        ins = tensor.matmul(
                                            psum[0:65, avo:avo + 512],
                                            v_p[:, kc * 130:kc * 130 + 65],
                                            attnT[:, slot:slot + 512],
                                            start=(kc == 0), stop=(kc == 15),
                                            skip_group_check=True)
                                        ins = tensor.matmul(
                                            psum[0:64, avo + 512:avo + 1024],
                                            ones64[:],
                                            attnT[:, slot + 512:slot + 1024],
                                            start=(kc == 0), stop=(kc == 15),
                                            skip_group_check=True)
                                        ins = tensor.matmul(
                                            psum[64:128, avo + 512:avo + 1024],
                                            v_p[:, kc * 130 + 65:kc * 130 + 129],
                                            attnT[:, slot + 512:slot + 1024],
                                            start=(kc == 0), stop=(kc == 15),
                                            skip_group_check=True)
                                        ins.then_inc(sm["sAV"], 1)
                            tensor.reg_add(rPVB, rPVB, 16)
                            tensor.reg_add(rKCB, rKCB, 128)
                    tensor.wait_ge(sm["sRC"], 32)
                    tensor.wait_ge(sm["sBL"], 31)
                    for mi in range(2):
                        bco = BCAST + mi * 512
                        ins = tensor.matmul(psum[0:64, bco:bco + 512],
                                            onesf[64:65, 0:64], rcp_sb[64:65, mi * 512:mi * 512 + 512],
                                            start=True, stop=True, skip_group_check=True)
                        ins = tensor.matmul(psum[64:128, bco:bco + 512],
                                            onesf[0:1, 0:64], rcp_sb[0:1, mi * 512:mi * 512 + 512],
                                            start=True, stop=True, skip_group_check=True)
                    ins.then_inc(sm["sBC"], 1)

                @blk.scalar
                def _(scalar):
                    rW = scalar.alloc_register("rW")
                    rKC = scalar.alloc_register("rKC")
                    scalar.reg_mov(rKC, 1)
                    with scalar.Fori(0, HP // 2) as hpi:
                        for hpar in range(2):
                            with scalar.Fori(0, 4) as qh:
                                scalar.reg_alu(rW, qh, 32, ALU.mult)
                                scalar.reg_alu(rW, rW, rKC, ALU.add)
                                for mat in range(2):
                                    for kc in range(16):
                                        scalar.wait_ge(sm["sST"], rW)
                                        scalar.wait_ge(sm["sAV"], rW)
                                        scalar.activation(
                                            attnT[:, (kc % 2) * 1024:(kc % 2) * 1024 + 1024],
                                            psum[:, ST_A:ST_A + 1024],
                                            AF.Exp).then_inc(sm["sEX"], 1)
                                        scalar.reg_add(rW, rW, 1)
                            scalar.reg_add(rKC, rKC, 128)

                @blk.vector
                def _(vector):
                    vector.memset(
                        v_p[:, :].rearrange("p (t x c) -> p t x c", t=16, x=2)[:, :, :, 64:65], 1.0)

                    def blend_v2(bq, vector=vector):
                        co = bq * QW
                        vector.tensor_copy(rb[:], psum[:, BCAST:BCAST + 1024])
                        vector.drain()
                        vector.tensor_mul(av_sb[0:64, 0:512], av_sb[0:64, 0:512], rb[0:64, 0:512])
                        vector.tensor_mul(av_sb[64:128, 512:1024], av_sb[64:128, 512:1024], rb[64:128, 0:512])
                        vector.tensor_mul(av_sb[0:64, 1024:1536], av_sb[0:64, 1024:1536], rb[0:64, 512:1024])
                        vector.tensor_mul(av_sb[64:128, 1536:2048], av_sb[64:128, 1536:2048], rb[64:128, 512:1024])
                        vector.drain()
                        vector.tensor_add(attn_out[0:64, co:co + QW],
                                          av_sb[0:64, 0:512], av_sb[0:64, 1024:1536])
                        vector.tensor_add(attn_out[64:128, co:co + QW],
                                          av_sb[64:128, 512:1024], av_sb[64:128, 1536:2048]
                                          ).then_inc(sm["sBL"], 1)
                        vector.drain()

                    for hp in range(HP):
                        PJ = hp * 19
                        if hp >= 1:
                            vector.wait_ge(sm["sST"], 128 * hp)
                            vector.wait_ge(sm["sAV"], 128 * hp + SEED["sAV"])

                        def vevac(t, hp=hp, vector=vector):
                            gidx = hp * 19 + (2 + t if t < 4 else (3 + t if t < 8 else 4 + t))
                            vector.wait_ge(sm["sPJ"], gidx)
                            vp = 2048 + (t % 2) * 512
                            vector.tensor_copy(
                                v_p[:, t * 130:t * 130 + 130].rearrange(
                                    "p (h d) -> p h d", h=2)[:, :, 0:64],
                                psum[:, vp:vp + 128].rearrange("p (h d) -> p h d", h=2),
                            ).then_inc(sm["sPV"], 1)

                        vector.wait_ge(sm["sPJ"], PJ + 1)
                        vector.tensor_scalar_mul(q_fm[:], psum[:, 0:2048], 0.125).then_inc(sm["sPE"], 1)
                        for t in range(4):
                            vevac(t)
                        vector.wait_ge(sm["sPJ"], PJ + 6)
                        vector.tensor_copy(k_fm[:], psum[:, 0:2048]).then_inc(sm["sPE"], 1)
                        for t in range(4, 8):
                            vevac(t)
                        vector.wait_ge(sm["sPJ"], PJ + 11)
                        vector.tensor_copy(kp_fm[:], psum[:, 0:2048]).then_inc(sm["sPE"], 1)
                        for t in range(8, 16):
                            vevac(t)
                        for qh in range(4):
                            bq = hp * 4 + qh
                            if bq >= 1:
                                vector.wait_ge(sm["sBC"], bq)
                                blend_v2(bq - 1)
                            vector.wait_ge(sm["sAV"], 16 * (2 * bq + 1) + SEED["sAV"])
                            vector.tensor_copy(av_sb[:, 0:1024],
                                               psum[:, AV_STD:AV_STD + 1024]).then_inc(sm["sAVE"], 1)
                            vector.wait_ge(sm["sAV"], 16 * (2 * bq + 2) + SEED["sAV"])
                            vector.tensor_copy(av_sb[:, 1024:2048],
                                               psum[:, AV_P:AV_P + 1024]).then_inc(sm["sAVE"], 1)
                            vector.drain()
                            vector.reciprocal(
                                rcp_sb[64:65, :].rearrange("p (m w) -> p m w", m=2, w=512),
                                av_sb[64:65, :].rearrange("p (m w) -> p m w", m=2, w=1024)[:, :, 0:512])
                            vector.reciprocal(
                                rcp_sb[0:1, :].rearrange("p (m w) -> p m w", m=2, w=512),
                                av_sb[0:1, :].rearrange("p (m w) -> p m w", m=2, w=1024)[:, :, 512:1024])
                            vector.drain()
                            vector.tensor_scalar_mul(rcp_sb[64:65, 0:512], rcp_sb[64:65, 0:512],
                                                     alpha_a[64:65, 0:1])
                            vector.tensor_scalar_mul(rcp_sb[0:1, 0:512], rcp_sb[0:1, 0:512],
                                                     alpha_a[0:1, 0:1])
                            vector.tensor_scalar_mul(rcp_sb[64:65, 512:1024], rcp_sb[64:65, 512:1024],
                                                     alpha_a[64:65, 1:2])
                            vector.tensor_scalar_mul(rcp_sb[0:1, 512:1024], rcp_sb[0:1, 512:1024],
                                                     alpha_a[0:1, 1:2]).then_inc(sm["sRC"], 1)
                    vector.wait_ge(sm["sBC"], 32)
                    blend_v2(31)

            if stage >= 2:
                @blk.tensor
                def _(tensor):
                    tensor.wait_ge(sm["sBL"], 32)
                    for mc in range(KC):
                        tensor.wait_ge(sm["wol%d" % (mc % 2)], 16 * (mc // 2 + 1))
                        if mc >= 2:
                            tensor.wait_ge(sm["sWOE"], mc - 1)
                        pc = (mc % 2) * 2048
                        for kcj in range(KC):
                            for nb in range(4):
                                ins = tensor.matmul(
                                    psum[:, pc + nb * 512:pc + nb * 512 + 512],
                                    wo_p[:, (mc % 2) * 1024 + kcj * 128:(mc % 2) * 1024 + kcj * 128 + 128],
                                    attn_out[:, kcj * T + nb * 512:kcj * T + nb * 512 + 512],
                                    start=(kcj == 0), stop=(kcj == KC - 1),
                                    skip_group_check=True)
                        ins.then_inc(sm["sWO"], 1)

                @blk.vector
                def _(vector):
                    for mc in range(KC):
                        vector.wait_ge(sm["sWO"], mc + 1)
                        pc = (mc % 2) * 2048
                        vector.tensor_add(xf32[:, mc * T:mc * T + T],
                                          xf32[:, mc * T:mc * T + T],
                                          psum[:, pc:pc + 2048]).then_inc(sm["sWOE"], 1)

            if stage >= 3:
                @blk.tensor
                def _(tensor):
                    tensor.wait_ge(sm["sWOE"], KC)
                    for tc in range(TC):
                        if tc >= 2:
                            tensor.wait_ge(sm["sL1C"], tc - 1)
                        pc = (tc % 2) * 1024
                        for cc in range(KC):
                            ins = tensor.transpose(
                                psum[:, pc + cc * 128:pc + cc * 128 + 128],
                                xf32[:, cc * T + tc * 128:cc * T + tc * 128 + 128], idf[:])
                        ins.then_inc(sm["sL1T"], 1)
                        tensor.wait_ge(sm["sL1N"], tc + 1)
                        if tc >= 2:
                            tensor.wait_ge(sm["sL1E"], tc - 1)
                        pb = 2048 + (tc % 2) * 1024
                        for cc in range(KC):
                            ins = tensor.transpose(
                                psum[:, pb + cc * 128:pb + cc * 128 + 128],
                                ysq[:, (tc % 2) * 1024 + cc * 128:(tc % 2) * 1024 + cc * 128 + 128], idf[:])
                        ins.then_inc(sm["sL1B"], 1)

                @blk.scalar
                def _(scalar):
                    for tc in range(TC):
                        scalar.wait_ge(sm["sLNa"], tc + 1)
                        scalar.activation(stat[:, 8:9], stat[:, 7:8], AF.Sqrt).then_inc(sm["sLNb"], 1)

                @blk.vector
                def _(vector):
                    for tc in range(TC):
                        sl = (tc % 2) * 1024
                        vector.wait_ge(sm["sL1T"], tc + 1)
                        if tc >= 2:
                            vector.wait_ge(sm["sL1B"], tc - 1)
                        vector.tensor_copy(ytm[:, sl:sl + 1024], psum[:, sl:sl + 1024]).then_inc(sm["sL1C"], 1)
                        vector.drain()
                        vector.reduce_sum(stat[:, 0:1], ytm[:, sl:sl + 1024], axis=AX.X)
                        vector.tensor_mul(ysq[:, sl:sl + 1024], ytm[:, sl:sl + 1024], ytm[:, sl:sl + 1024])
                        vector.drain()
                        vector.reduce_sum(stat[:, 1:2], ysq[:, sl:sl + 1024], axis=AX.X)
                        vector.drain()
                        vector.tensor_scalar_mul(stat[:, 2:3], stat[:, 0:1], 1.0 / C)
                        vector.tensor_scalar_mul(stat[:, 3:4], stat[:, 1:2], 1.0 / C)
                        vector.drain()
                        vector.tensor_mul(stat[:, 4:5], stat[:, 2:3], stat[:, 2:3])
                        vector.drain()
                        vector.tensor_sub(stat[:, 5:6], stat[:, 3:4], stat[:, 4:5])
                        vector.drain()
                        vector.tensor_scalar_add(stat[:, 6:7], stat[:, 5:6], 1e-5)
                        vector.drain()
                        vector.reciprocal(stat[:, 7:8], stat[:, 6:7]).then_inc(sm["sLNa"], 1)
                        vector.wait_ge(sm["sLNb"], tc + 1)
                        vector.drain()
                        vector.tensor_scalar(ysq[:, sl:sl + 1024], ytm[:, sl:sl + 1024],
                                             stat[:, 2:3], stat[:, 8:9],
                                             ALU.subtract, ALU.mult).then_inc(sm["sL1N"], 1)
                        vector.wait_ge(sm["sL1B"], tc + 1)
                        pb = 2048 + (tc % 2) * 1024
                        pin = psum[:, pb:pb + 1024].rearrange("p (cc t) -> p cc t", cc=KC)
                        vector.tensor_copy(
                            xf32[:, :].rearrange("p (cc t) -> p cc t", cc=KC)[:, :, tc * 128:tc * 128 + 128],
                            pin)
                        vector.tensor_copy(
                            xbf[:, :].rearrange("p (cc t) -> p cc t", cc=KC)[:, :, tc * 128:tc * 128 + 128],
                            pin).then_inc(sm["sL1E"], 1)
                        vector.drain()

            @blk.sync
            def _(sync):
                tap(sync, "xf32", xf32[:], "sXE", TC)
                tap(sync, "qfm", q_fm[:], "sPE", 22)
                tap(sync, "kfm", k_fm[:], "sPE", 23)
                tap(sync, "vp", v_p[:], "sPV", 128)
                tap(sync, "attn_out", attn_out[:], "sBL", 32)
                tap(sync, "x2", xf32[:], "sL1E", TC)
                tap(sync, "x2bf", xbf[:], "sL1E", TC)

        fctx = contextlib.ExitStack()
        with fctx:
            w1g = fctx.enter_context(nc.sbuf_tensor("w1g", [128, 2 * KC * FG], BF16))
            w3g = fctx.enter_context(nc.sbuf_tensor("w3g", [128, 2 * KC * FG], BF16))
            w2g = fctx.enter_context(nc.sbuf_tensor("w2g", [128, 2 * FCG * C], BF16))
            wgb = fctx.enter_context(nc.sbuf_tensor("wgb", [128, KC * 8], BF16))
            wbc_all = fctx.enter_context(nc.sbuf_tensor("wbc_all", [128, E * T], BF16))
            wbc = fctx.enter_context(nc.sbuf_tensor("wbc", [128, T], BF16))
            h_w = fctx.enter_context(nc.sbuf_tensor("h_w", [128, 2 * FCG * TTW], BF16))
            s1 = fctx.enter_context(nc.sbuf_tensor("s1", [128, 2 * TTW], BF16))
            lg_sb = ysq
            combT = ysq
            et_sb = ytm[:, 0:128]
            comb_tm = ytm[:, 128:256]
            rwk = ytm[:, 256:304]

            if stage >= 4:
                @blk.gpsimd
                def _(g):
                    g.wait_ge(sm["sBL"], 32)
                    g.dma_start(out=wgb[:].rearrange("p (kc e) -> p kc e", kc=KC),
                                in_=wg_d.rearrange("(kc p) e -> p kc e", p=128)).then_inc(sm["wgl"], 16)

                @blk.tensor
                def _(tensor):
                    tensor.wait_ge(sm["sL1E"], TC)
                    tensor.wait_ge(sm["wgl"], 16)
                    for nb in range(4):
                        for kcj in range(KC):
                            ins = tensor.matmul(
                                psum[0:8, nb * 512:nb * 512 + 512],
                                wgb[:, kcj * 8:kcj * 8 + 8],
                                xbf[:, kcj * T + nb * 512:kcj * T + nb * 512 + 512],
                                start=(kcj == 0), stop=(kcj == KC - 1),
                                skip_group_check=True)
                    ins.then_inc(sm["sRT"], 1)
                    tensor.wait_ge(sm["sRTE"], 1)
                    for c in range(16):
                        ins = tensor.transpose(psum[:, 2048 + c * 8:2048 + c * 8 + 8],
                                               lg_sb[0:8, c * 128:c * 128 + 128], idf[0:8, 0:8])
                    ins.then_inc(sm["sRT"], 1)
                    tensor.wait_ge(sm["sRTT"], 1)
                    for c in range(16):
                        ins = tensor.transpose(psum[0:8, c * 128:c * 128 + 128],
                                               comb_tm[:, c * 8:c * 8 + 8], idf[:])
                    ins.then_inc(sm["sRT"], 1)

                @blk.scalar
                def _(scalar):
                    scalar.wait_ge(sm["sRT"], 2)
                    scalar.activation(et_sb[:], psum[:, 2048:2176], AF.Exp).then_inc(sm["sREX"], 1)

                @blk.vector
                def _(vector):
                    vector.wait_ge(sm["sRT"], 1)
                    vector.tensor_copy(lg_sb[0:8, 0:2048], psum[0:8, 0:2048]).then_inc(sm["sRTE"], 1)
                    vector.wait_ge(sm["sREX"], 1)
                    for c in range(16):
                        ec = et_sb[:, c * 8:c * 8 + 8]
                        vector.reduce_max(rwk[:, 0:1], ec, axis=AX.X)
                        vector.reduce_sum(rwk[:, 4:5], ec, axis=AX.X)
                        vector.drain()
                        vector.tensor_scalar(rwk[:, 8:16], ec, rwk[:, 0:1], None, ALU.is_equal)
                        vector.drain()
                        vector.scalar_tensor_tensor(rwk[:, 16:24], rwk[:, 8:16], -1000.0, iotaP[:],
                                                    ALU.mult, ALU.add)
                        vector.drain()
                        vector.tensor_reduce(rwk[:, 1:2], rwk[:, 16:24], axis=AX.X, op=ALU.min)
                        vector.drain()
                        vector.tensor_scalar(rwk[:, 8:16], iota8[:], rwk[:, 1:2], None, ALU.is_equal)
                        vector.drain()
                        vector.scalar_tensor_tensor(rwk[:, 16:24], rwk[:, 8:16], -1e9, ec,
                                                    ALU.mult, ALU.add)
                        vector.drain()
                        vector.reduce_max(rwk[:, 2:3], rwk[:, 16:24], axis=AX.X)
                        vector.drain()
                        vector.tensor_scalar(rwk[:, 24:32], rwk[:, 16:24], rwk[:, 2:3], None, ALU.is_equal)
                        vector.drain()
                        vector.scalar_tensor_tensor(rwk[:, 32:40], rwk[:, 24:32], -1000.0, iotaP[:],
                                                    ALU.mult, ALU.add)
                        vector.drain()
                        vector.tensor_reduce(rwk[:, 3:4], rwk[:, 32:40], axis=AX.X, op=ALU.min)
                        vector.drain()
                        vector.tensor_scalar(rwk[:, 24:32], iota8[:], rwk[:, 3:4], None, ALU.is_equal)
                        vector.tensor_add(rwk[:, 5:6], rwk[:, 0:1], rwk[:, 2:3])
                        vector.drain()
                        vector.scalar_tensor_tensor(rwk[:, 6:7], rwk[:, 4:5], 1e-8, rwk[:, 5:6],
                                                    ALU.mult, ALU.add)
                        vector.drain()
                        vector.reciprocal(rwk[:, 7:8], rwk[:, 6:7])
                        vector.drain()
                        vector.tensor_mul(rwk[:, 40:41], rwk[:, 0:1], rwk[:, 7:8])
                        vector.tensor_mul(rwk[:, 41:42], rwk[:, 2:3], rwk[:, 7:8])
                        vector.drain()
                        vector.tensor_scalar(rwk[:, 32:40], rwk[:, 8:16], rwk[:, 40:41], None, ALU.mult)
                        vector.drain()
                        vector.scalar_tensor_tensor(comb_tm[:, c * 8:c * 8 + 8],
                                                    rwk[:, 24:32], rwk[:, 41:42], rwk[:, 32:40],
                                                    ALU.mult, ALU.add)
                        vector.drain()
                    vector.nop().then_inc(sm["sRTT"], 1)
                    vector.wait_ge(sm["sRT"], 3)
                    vector.tensor_copy(combT[0:8, 0:2048], psum[0:8, 0:2048]).then_inc(sm["sRTC"], 1)

            if stage >= 5:
                @blk.gpsimd
                def _(g):
                    for n in range(E * G):
                        e, gg = n // G, n % G
                        par = n % 2
                        if gg == 0:
                            g.wait_ge(sm["sWBE"], e + 1)
                            if e >= 1:
                                g.wait_ge(sm["sHw"], 256 * e + SEED["sHw"])
                            g.dma_start(out=wbc[:], in_=wbc_all[:, e * T:(e + 1) * T]
                                        ).then_inc(sm["sWC"], 16)
                        if n < 2:
                            g.wait_ge(sm["sBL"], 32)
                        else:
                            g.wait_ge(sm["sEv"], 16 * ((n - 2) // 2 + 1))
                        g.dma_start(
                            out=w1g[:, par * KC * FG:(par + 1) * KC * FG].rearrange(
                                "p (kc f) -> p kc f", kc=KC),
                            in_=w1_d[e].rearrange("(kc p) f -> p kc f", p=128)[:, :, gg * FG:(gg + 1) * FG],
                        ).then_inc(sm["wf%d" % par], 16)
                        g.dma_start(
                            out=w3g[:, par * KC * FG:(par + 1) * KC * FG].rearrange(
                                "p (kc f) -> p kc f", kc=KC),
                            in_=w3_d[e].rearrange("(kc p) f -> p kc f", p=128)[:, :, gg * FG:(gg + 1) * FG],
                        ).then_inc(sm["wf%d" % par], 16)
                        g.dma_start(
                            out=w2g[:, par * FCG * C:(par + 1) * FCG * C].rearrange(
                                "p (fc c) -> p fc c", fc=FCG),
                            in_=w2_d[e, gg * FG:(gg + 1) * FG, :].rearrange("(fc p) c -> p fc c", p=128),
                        ).then_inc(sm["wf%d" % par], 16)

                @blk.tensor
                def _(tensor):
                    tensor.wait_ge(sm["sRTC"], 1)
                    for e in range(E):
                        if e >= 1:
                            tensor.wait_ge(sm["sWBE"], e)
                        for nb in range(4):
                            ins = tensor.matmul(psum[:, nb * 512:nb * 512 + 512],
                                                selb[:, e * 128:e * 128 + 128],
                                                combT[0:8, nb * 512:nb * 512 + 512],
                                                start=True, stop=True, skip_group_check=True)
                        ins.then_inc(sm["sWB"], 1)
                    tensor.wait_ge(sm["sWBE"], E)
                    rWF = tensor.alloc_register("rWF")
                    rSL = tensor.alloc_register("rSL")
                    rHW = tensor.alloc_register("rHW")
                    rEV = tensor.alloc_register("rEV")
                    rTF = tensor.alloc_register("rTF")
                    tensor.reg_mov(rWF, 0)
                    tensor.reg_mov(rSL, 0)
                    tensor.reg_mov(rHW, 0)
                    tensor.reg_mov(rEV, 0)
                    with tensor.Fori(0, E * G // 2) as p:
                        tensor.reg_add(rWF, rWF, 48)
                        tensor.wait_ge(sm["wf0"], rWF)
                        tensor.wait_ge(sm["wf1"], rWF)
                        for tt in range(NTT):
                            tto = tt * TTW
                            for par in range(2):
                                wo1 = par * KC * FG
                                wo2 = par * FCG * C
                                hwo = par * FCG * TTW
                                for fc in range(FCG):
                                    hA = HPS + (fc % 2) * 1024
                                    hB = hA + 512
                                    tensor.reg_add(rSL, rSL, 1)
                                    tensor.wait_ge(sm["sSl"], rSL)
                                    for kcj in range(KC):
                                        ins = tensor.matmul(
                                            psum[:, hA:hA + TTW],
                                            w1g[:, wo1 + kcj * FG + fc * 128:wo1 + kcj * FG + fc * 128 + 128],
                                            xbf[:, kcj * T + tto:kcj * T + tto + TTW],
                                            start=(kcj == 0), stop=(kcj == KC - 1),
                                            skip_group_check=True)
                                    ins.then_inc(sm["sA"], 1)
                                    tensor.reg_add(rHW, rHW, 1)
                                    tensor.wait_ge(sm["sHw"], rHW)
                                    for kcj in range(KC):
                                        ins = tensor.matmul(
                                            psum[:, hB:hB + TTW],
                                            w3g[:, wo1 + kcj * FG + fc * 128:wo1 + kcj * FG + fc * 128 + 128],
                                            xbf[:, kcj * T + tto:kcj * T + tto + TTW],
                                            start=(kcj == 0), stop=(kcj == KC - 1),
                                            skip_group_check=True)
                                    ins.then_inc(sm["sB"], 1)
                                tensor.reg_alu(rTF, rHW, 2, ALU.add)
                                tensor.wait_ge(sm["sHw"], rTF)
                                tensor.wait_ge(sm["sEv"], rEV)
                                for fc in range(FCG):
                                    for m in range(KC):
                                        ins = tensor.matmul(
                                            psum[:, m * TTW:m * TTW + TTW],
                                            w2g[:, wo2 + fc * C + m * 128:wo2 + fc * C + m * 128 + 128],
                                            h_w[:, hwo + fc * TTW:hwo + fc * TTW + TTW],
                                            start=(fc == 0 and m % 2 == 0), stop=(fc == FCG - 1),
                                            skip_group_check=True)
                                ins.then_inc(sm["sW2"], 1)
                                tensor.reg_add(rEV, rEV, 1)

                @blk.scalar
                def _(scalar):
                    rSA = scalar.alloc_register("rSA")
                    rHW = scalar.alloc_register("rHW2")
                    scalar.reg_mov(rSA, 0)
                    scalar.reg_mov(rHW, 0)
                    with scalar.Fori(0, E * G // 2) as p:
                        for tt in range(NTT):
                            for par in range(2):
                                for fc in range(FCG):
                                    scalar.reg_add(rSA, rSA, 1)
                                    scalar.wait_ge(sm["sA"], rSA)
                                    scalar.reg_add(rHW, rHW, 1)
                                    scalar.wait_ge(sm["sHw"], rHW)
                                    hA = HPS + (fc % 2) * 1024
                                    scalar.activation(s1[:, (fc % 2) * TTW:(fc % 2) * TTW + TTW],
                                                      psum[:, hA:hA + TTW],
                                                      AF.Silu).then_inc(sm["sSl"], 1)

                @blk.vector
                def _(vector):
                    for e in range(E):
                        vector.wait_ge(sm["sWB"], e + 1)
                        vector.tensor_copy(wbc_all[:, e * T:(e + 1) * T],
                                           psum[:, 0:2048]).then_inc(sm["sWBE"], 1)
                    rB = vector.alloc_register("fB")
                    rSl = vector.alloc_register("fSl")
                    rW2a = vector.alloc_register("fW2a")
                    rW2b = vector.alloc_register("fW2b")
                    rWC = vector.alloc_register("fWC")
                    rVT = vector.alloc_register("fVT")
                    vector.reg_mov(rB, 0)
                    vector.reg_mov(rSl, 2)
                    vector.reg_mov(rW2a, 0)
                    vector.reg_mov(rW2b, 2)
                    vector.reg_mov(rWC, 0)
                    with vector.Fori(0, E * G // 2) as p:
                        vector.reg_alu(rVT, p, 4, ALU.mod)
                        with vector.If_eq(rVT, 0):
                            vector.reg_add(rWC, rWC, 16)
                        vector.wait_ge(sm["sWC"], rWC)
                        for tt in range(NTT):
                            tto = tt * TTW
                            for par in range(2):
                                hwo = par * FCG * TTW
                                vector.wait_ge(sm["sW2"], rW2a)
                                vector.reg_add(rW2a, rW2a, 1)
                                for fc in range(FCG):
                                    hB = HPS + (fc % 2) * 1024 + 512
                                    vector.reg_add(rB, rB, 1)
                                    vector.wait_ge(sm["sB"], rB)
                                    vector.tensor_mul(psum[:, hB:hB + TTW], psum[:, hB:hB + TTW],
                                                      wbc[:, tto:tto + TTW])
                                    vector.drain()
                                    vector.reg_add(rSl, rSl, 1)
                                    vector.wait_ge(sm["sSl"], rSl)
                                    vector.tensor_mul(
                                        h_w[:, hwo + fc * TTW:hwo + fc * TTW + TTW],
                                        s1[:, (fc % 2) * TTW:(fc % 2) * TTW + TTW],
                                        psum[:, hB:hB + TTW]).then_inc(sm["sHw"], 1)
                                vector.wait_ge(sm["sW2"], rW2b)
                                vector.reg_add(rW2b, rW2b, 1)
                                vector.tensor_add(
                                    xf32[:, :].rearrange("p (cc t) -> p cc t", cc=KC)[:, :, tto:tto + TTW],
                                    xf32[:, :].rearrange("p (cc t) -> p cc t", cc=KC)[:, :, tto:tto + TTW],
                                    psum[:, 0:2048].rearrange("p (m t) -> p m t", m=KC),
                                ).then_inc(sm["sEv"], 1)

            if stage >= 6:
                @blk.tensor
                def _(tensor):
                    tensor.wait_ge(sm["sEv"], 512)
                    for tc in range(TC):
                        if tc >= 2:
                            tensor.wait_ge(sm["sFC"], tc - 1)
                        pc = (tc % 2) * 1024
                        for cc in range(KC):
                            ins = tensor.transpose(
                                psum[:, pc + cc * 128:pc + cc * 128 + 128],
                                xf32[:, cc * T + tc * 128:cc * T + tc * 128 + 128], idf[:])
                        ins.then_inc(sm["sFT"], 1)

                @blk.scalar
                def _(scalar):
                    for tc in range(TC):
                        scalar.wait_ge(sm["sLNa"], TC + tc + 1)
                        scalar.activation(stat[:, 8:9], stat[:, 7:8], AF.Sqrt).then_inc(sm["sLNb"], 1)

                @blk.vector
                def _(vector):
                    for tc in range(TC):
                        sl = (tc % 2) * 1024
                        vector.wait_ge(sm["sFT"], tc + 1)
                        vector.tensor_copy(ytm[:, sl:sl + 1024], psum[:, sl:sl + 1024]).then_inc(sm["sFC"], 1)
                        vector.drain()
                        vector.reduce_sum(stat[:, 0:1], ytm[:, sl:sl + 1024], axis=AX.X)
                        if tc >= 2:
                            vector.wait_ge(sm["sty%d" % (tc % 2)], 16 * (tc // 2))
                        vector.tensor_mul(ysq[:, sl:sl + 1024], ytm[:, sl:sl + 1024], ytm[:, sl:sl + 1024])
                        vector.drain()
                        vector.reduce_sum(stat[:, 1:2], ysq[:, sl:sl + 1024], axis=AX.X)
                        vector.drain()
                        vector.tensor_scalar_mul(stat[:, 2:3], stat[:, 0:1], 1.0 / C)
                        vector.tensor_scalar_mul(stat[:, 3:4], stat[:, 1:2], 1.0 / C)
                        vector.drain()
                        vector.tensor_mul(stat[:, 4:5], stat[:, 2:3], stat[:, 2:3])
                        vector.drain()
                        vector.tensor_sub(stat[:, 5:6], stat[:, 3:4], stat[:, 4:5])
                        vector.drain()
                        vector.tensor_scalar_add(stat[:, 6:7], stat[:, 5:6], 1e-5)
                        vector.drain()
                        vector.reciprocal(stat[:, 7:8], stat[:, 6:7]).then_inc(sm["sLNa"], 1)
                        vector.wait_ge(sm["sLNb"], TC + tc + 1)
                        vector.drain()
                        vector.tensor_scalar(ysq[:, sl:sl + 1024], ytm[:, sl:sl + 1024],
                                             stat[:, 2:3], stat[:, 8:9],
                                             ALU.subtract, ALU.mult).then_inc(sm["sFN"], 1)
                        vector.drain()

                @blk.sync
                def _(sync):
                    for tc in range(TC):
                        sync.wait_ge(sm["sFN"], tc + 1)
                        sync.dma_start(out=y_d[tc * 128:(tc + 1) * 128, :],
                                       in_=ysq[:, (tc % 2) * 1024:(tc % 2) * 1024 + 1024]
                                       ).then_inc(sm["sty%d" % (tc % 2)], 16)

            @blk.sync
            def _(sync):
                tap(sync, "combT", combT[0:8, 0:T], "sRTC", 1)
                tap(sync, "acc", xf32[:], "sEv", 512)
                if n_stores[0]:
                    sync.wait_ge(sm["st"], 16 * n_stores[0])
                if stage >= 6:
                    sync.wait_ge(sm["sty0"], 16 * TC // 2)
                    sync.wait_ge(sm["sty1"], 16 * TC // 2)

    return nc, tap_d


def _prep_inputs(inputs):
    f32 = np.float32
    base = {
        "wqT": np.ascontiguousarray(np.asarray(inputs["L0_attn_W_Q"], dtype=f32).T),
        "wkT": np.ascontiguousarray(np.asarray(inputs["L0_attn_W_K"], dtype=f32).T),
        "wkpT": np.ascontiguousarray(np.asarray(inputs["L0_attn_W_K_prime"], dtype=f32).T),
        "wvT": np.ascontiguousarray(np.asarray(inputs["L0_attn_W_V"], dtype=f32).T),
        "woT": np.ascontiguousarray(np.asarray(inputs["L0_attn_W_O"], dtype=f32).T),
        "alpha": np.asarray(inputs["L0_attn_alpha"], dtype=f32).reshape(1, 1),
        "wg": np.ascontiguousarray(np.asarray(inputs["L0_ffn_Wg"], dtype=f32)),
        "w1": np.ascontiguousarray(np.asarray(inputs["L0_ffn_W1"], dtype=f32)),
        "w2": np.ascontiguousarray(np.asarray(inputs["L0_ffn_W2"], dtype=f32)),
        "w3": np.ascontiguousarray(np.asarray(inputs["L0_ffn_W3"], dtype=f32)),
    }
    x = np.asarray(inputs["L0_attn_x"], dtype=f32)
    in_maps = []
    for b in range(8):
        m = dict(base)
        m["x"] = np.ascontiguousarray(x[b])
        in_maps.append(m)
    return in_maps


def kernel(**inputs):
    nc = bass.Bass("TRN2", target_bir_lowering=False)
    build(nc)
    in_maps = _prep_inputs(inputs)
    res = run_bass_kernel_spmd(nc, in_maps, core_ids=list(range(8)))
    return np.stack([res.results[b]["y"] for b in range(8)], axis=0)


# version 02
# speedup vs baseline: 1.3471x; 1.3471x over previous
import contextlib
import numpy as np
import ml_dtypes
import concourse.bass as bass
import concourse.mybir as mybir
from concourse.bass import ds
from concourse.bass_utils import run_bass_kernel_spmd

F32 = mybir.dt.float32
BF16 = mybir.dt.bfloat16
AF = mybir.ActivationFunctionType
ALU = mybir.AluOpType
AX = mybir.AxisListType

T, C, KC, TC = 2048, 1024, 8, 16
HP, DH, QW = 8, 64, 512
E, DFF, FG, G, FCG, NTT, TTW = 8, 4096, 512, 8, 4, 8, 256

ST_A, ST_B = 0, 512
AV_STD, AV_P = 1024, 2048
BCAST = 3072
HPS = 2048

SEED = {"sAV": 2, "sAVE": 2, "sSl": 2, "sHw": 2, "sW2": 1, "sEX": 1}


def build(nc, stage=99, taps=()):
    x_d = nc.dram_tensor("x", [T, C], F32, kind="ExternalInput").ap()
    wqT = nc.dram_tensor("wqT", [C, C], F32, kind="ExternalInput").ap()
    wkT = nc.dram_tensor("wkT", [C, C], F32, kind="ExternalInput").ap()
    wkpT = nc.dram_tensor("wkpT", [C, C], F32, kind="ExternalInput").ap()
    wvT = nc.dram_tensor("wvT", [C, C], F32, kind="ExternalInput").ap()
    woT = nc.dram_tensor("woT", [C, C], F32, kind="ExternalInput").ap()
    alpha_d = nc.dram_tensor("alpha", [1, 1], F32, kind="ExternalInput").ap()
    wg_d = nc.dram_tensor("wg", [C, E], F32, kind="ExternalInput").ap()
    w1_d = nc.dram_tensor("w1", [E, C, DFF], F32, kind="ExternalInput").ap()
    w2_d = nc.dram_tensor("w2", [E, DFF, C], F32, kind="ExternalInput").ap()
    w3_d = nc.dram_tensor("w3", [E, C, DFF], F32, kind="ExternalInput").ap()
    y_d = nc.dram_tensor("y", [T, C], F32, kind="ExternalOutput").ap()
    tap_d = {}
    for tname, tshape, tdt in taps:
        tap_d[tname] = nc.dram_tensor("tap_" + tname, list(tshape), tdt,
                                      kind="ExternalOutput").ap()

    id_f32 = nc.inline_tensor(np.eye(128, dtype=np.float32), "c_idf").ap()
    ones_fc = nc.inline_tensor(np.ones((128, 128), np.float32), "c_onesf").ap()
    ones_f1 = nc.inline_tensor(np.ones((128, 1), np.float32), "c_onesf1").ap()
    sel_np = np.zeros((8, 8 * 128), np.float32)
    for e in range(8):
        sel_np[e, e * 128:(e + 1) * 128] = 1.0
    sel_c = nc.inline_tensor(sel_np, "c_sel8").ap()
    iota_np = np.tile(np.arange(8, dtype=np.float32)[None, :], (128, 1))
    iota_c = nc.inline_tensor(iota_np, "c_iota8").ap()
    iotp_c = nc.inline_tensor(iota_np + 1000.0, "c_iotaP").ap()

    ctx = contextlib.ExitStack()
    with ctx:
        xf32 = ctx.enter_context(nc.sbuf_tensor("xf32", [128, KC * T], F32))
        xbf = ctx.enter_context(nc.sbuf_tensor("xbf", [128, KC * T], BF16))
        ytm = ctx.enter_context(nc.sbuf_tensor("ytm", [128, 2 * 1024], F32))
        ysq = ctx.enter_context(nc.sbuf_tensor("ysq", [128, 2 * 1024], F32))
        idf = ctx.enter_context(nc.sbuf_tensor("idf", [128, 128], F32))
        idb = ctx.enter_context(nc.sbuf_tensor("idb", [128, 128], BF16))
        onesf = ctx.enter_context(nc.sbuf_tensor("onesf", [128, 128], F32))
        onesb = ctx.enter_context(nc.sbuf_tensor("onesb", [128, 1], BF16))
        ones64 = ctx.enter_context(nc.sbuf_tensor("ones64", [128, 64], BF16))
        selb = ctx.enter_context(nc.sbuf_tensor("selb", [8, 8 * 128], F32))
        iota8 = ctx.enter_context(nc.sbuf_tensor("iota8", [128, 8], F32))
        iotaP = ctx.enter_context(nc.sbuf_tensor("iotaP", [128, 8], F32))
        alpha_a = ctx.enter_context(nc.sbuf_tensor("alpha_a", [128, 2], F32))
        stat = ctx.enter_context(nc.sbuf_tensor("stat", [128, 10], F32))
        psum = ctx.enter_context(nc.psum_tensor("arena", [128, 4096], F32))

        sem_names = [
            "ldx0", "ldx1", "cl", "clg", "sMS", "sXT", "sXE", "sALP", "sALPE",
            "watq", "watk", "watkp", "watv", "wol0", "wol1", "wgl", "sPJ", "sPE", "sPV",
            "sST", "sEX", "sAV", "sAVE", "sRC", "sBC", "sBL", "sVM",
            "sWO", "sWOE",
            "sL1T", "sL1C", "sLNa", "sLNb", "sL1N", "sL1B", "sL1E",
            "sRT", "sRTE", "sREX", "sRTT", "sRTC",
            "sWB", "sWBE", "sWC", "wf0", "wf1", "sA", "sB", "sSl", "sHw", "sW2", "sEv",
            "sFT", "sFC", "sFN", "st", "sty0", "sty1",
        ]
        sm = {n: ctx.enter_context(nc.semaphore(n)) for n in sem_names}
        blk = ctx.enter_context(nc.Block())

        n_stores = [0]

        def tap(sync, name, src_ap, wait_sem, wait_val):
            if name in tap_d:
                sync.wait_ge(sm[wait_sem], wait_val)
                sync.dma_start(out=tap_d[name], in_=src_ap).then_inc(sm["st"], 16)
                n_stores[0] += 1

        @blk.sync
        def _(sync):
            for dst, src in ((idf[:], id_f32), (onesf[:], ones_fc),
                             (selb[:], sel_c), (iota8[:], iota_c),
                             (iotaP[:], iotp_c), (alpha_a[0:1, 0:1], alpha_d)):
                sync.dma_start(out=dst, in_=src).then_inc(sm["cl"], 16)
            for tc in range(TC):
                if tc >= 2:
                    sync.wait_ge(sm["sXT"], tc - 1)
                sync.dma_start(out=ytm[:, (tc % 2) * 1024:(tc % 2) * 1024 + 1024],
                               in_=x_d[tc * 128:(tc + 1) * 128, :]).then_inc(sm["ldx%d" % (tc % 2)], 16)

        @blk.gpsimd
        def _(g):
            for s, v in SEED.items():
                g.sem_inc(sm[s], v)
            g.dma_start(out=idb[:], in_=id_f32).then_inc(sm["clg"], 16)
            g.dma_start(out=onesb[:], in_=ones_f1).then_inc(sm["clg"], 16)
            g.dma_start(out=ones64[:], in_=ones_fc[:, 0:64] if hasattr(ones_fc, '__getitem__') else ones_fc).then_inc(sm["clg"], 16)

        @blk.tensor
        def _(tensor):
            tensor.wait_ge(sm["cl"], 16 * 6)
            tensor.wait_ge(sm["clg"], 16 * 3)
            tensor.wait_ge(sm["sMS"], 1)
            tensor.matmul(psum[:, 2048:2049], onesf[0:1, 0:128], alpha_a[0:1, 0:1],
                          start=True, stop=True, skip_group_check=True).then_inc(sm["sALP"], 1)
            for tc in range(TC):
                tensor.wait_ge(sm["ldx%d" % (tc % 2)], 16 * (tc // 2 + 1))
                if tc >= 2:
                    tensor.wait_ge(sm["sXE"], tc - 1)
                pc = (tc % 2) * 1024
                for cc in range(KC):
                    ins = tensor.transpose(psum[:, pc + cc * 128:pc + cc * 128 + 128],
                                           ytm[:, (tc % 2) * 1024 + cc * 128:(tc % 2) * 1024 + cc * 128 + 128],
                                           idf[:])
                ins.then_inc(sm["sXT"], 1)

        @blk.vector
        def _(vector):
            vector.memset(psum[:, :], 0.0).then_inc(sm["sMS"], 1)
            vector.wait_ge(sm["sALP"], 1)
            vector.tensor_copy(alpha_a[:, 0:1], psum[:, 2048:2049])
            vector.drain()
            vector.tensor_scalar(alpha_a[:, 1:2], alpha_a[:, 0:1], -1.0, 1.0,
                                 ALU.mult, ALU.add).then_inc(sm["sALPE"], 1)
            vector.drain()
            for tc in range(TC):
                vector.wait_ge(sm["sXT"], tc + 1)
                pc = (tc % 2) * 1024
                pin = psum[:, pc:pc + 1024].rearrange("p (cc t) -> p cc t", cc=KC)
                vector.tensor_copy(
                    xf32[:, :].rearrange("p (cc t) -> p cc t", cc=KC)[:, :, tc * 128:tc * 128 + 128],
                    pin)
                vector.tensor_copy(
                    xbf[:, :].rearrange("p (cc t) -> p cc t", cc=KC)[:, :, tc * 128:tc * 128 + 128],
                    pin).then_inc(sm["sXE"], 1)

        actx = contextlib.ExitStack()
        with actx:
            wq_p = actx.enter_context(nc.sbuf_tensor("wq_p", [128, 2 * 1024], BF16))
            wk_p = actx.enter_context(nc.sbuf_tensor("wk_p", [128, 2 * 1024], BF16))
            wkp_p = actx.enter_context(nc.sbuf_tensor("wkp_p", [128, 2 * 1024], BF16))
            wv_p = actx.enter_context(nc.sbuf_tensor("wv_p", [128, 2 * 1024], BF16))
            q_fm = actx.enter_context(nc.sbuf_tensor("q_fm", [128, T], BF16))
            k_fm = actx.enter_context(nc.sbuf_tensor("k_fm", [128, T], BF16))
            kp_fm = actx.enter_context(nc.sbuf_tensor("kp_fm", [128, T], BF16))
            v_p = actx.enter_context(nc.sbuf_tensor("v_p", [128, 16 * 130], BF16))
            attnT = actx.enter_context(nc.sbuf_tensor("attnT", [128, 2 * 1024], BF16))
            av_sb = actx.enter_context(nc.sbuf_tensor("av_sb", [128, 2048], F32))
            rcp_sb = actx.enter_context(nc.sbuf_tensor("rcp_sb", [128, 1024], F32))
            rb = actx.enter_context(nc.sbuf_tensor("rb", [128, 1024], F32))
            wo_p = actx.enter_context(nc.sbuf_tensor("wo_p", [128, 2 * 1024], BF16))
            attn_out = actx.enter_context(nc.sbuf_tensor("attn_out", [128, HP * T], BF16))

            if stage >= 1:
                @blk.gpsimd
                def _(g):
                    for hp in range(HP):
                        par = hp % 2
                        kinds = (("watq", wq_p, wqT, 1), ("watk", wk_p, wkT, 6),
                                 ("watkp", wkp_p, wkpT, 11), ("watv", wv_p, wvT, 19))
                        for wsem, wten, wsrc, gate in kinds:
                            if hp >= 1:
                                g.wait_ge(sm["sPJ"], 19 * (hp - 1) + gate)
                            g.dma_start(
                                out=wten[:, par * 1024:(par + 1) * 1024].rearrange(
                                    "p (kc c) -> p kc c", kc=KC),
                                in_=wsrc.rearrange("(kc p) c -> p kc c", p=128)[:, :, hp * 128:(hp + 1) * 128],
                            ).then_inc(sm[wsem], 16)
                    if stage >= 2:
                        for mc in range(KC):
                            if mc >= 2:
                                g.wait_ge(sm["sWO"], mc - 1)
                            g.dma_start(
                                out=wo_p[:, (mc % 2) * 1024:(mc % 2) * 1024 + 1024].rearrange(
                                    "p (kc c) -> p kc c", kc=KC),
                                in_=woT.rearrange("(kc p) c -> p kc c", p=128)[:, :, mc * 128:(mc + 1) * 128],
                            ).then_inc(sm["wol%d" % (mc % 2)], 16)

                @blk.tensor
                def _(tensor):
                    tensor.wait_ge(sm["sXE"], TC)
                    tensor.wait_ge(sm["sALPE"], 1)
                    rWAT = tensor.alloc_register("rWAT")
                    rPVB = tensor.alloc_register("rPVB")
                    rT1 = tensor.alloc_register("rT1")
                    rPEW = tensor.alloc_register("rPEW")
                    rPEk = tensor.alloc_register("rPEk")
                    rKCB = tensor.alloc_register("rKCB")
                    rEX = tensor.alloc_register("rEX")
                    rAVE = tensor.alloc_register("rAVE")
                    rRC = tensor.alloc_register("rRC")
                    rT2 = tensor.alloc_register("rT2")
                    tensor.reg_mov(rWAT, 0)
                    tensor.reg_mov(rPVB, 0)
                    tensor.reg_mov(rPEW, 0)
                    tensor.reg_mov(rKCB, 0)
                    tensor.reg_mov(rAVE, 0)
                    tensor.reg_mov(rRC, 0)
                    with tensor.Fori(0, HP // 2) as hpi:
                        for hpar in range(2):
                            po = hpar * 1024
                            tensor.reg_alu(rT1, rKCB, 1, ALU.add)
                            tensor.wait_ge(sm["sEX"], rT1)
                            tensor.reg_alu(rT2, rAVE, 2, ALU.add)
                            tensor.wait_ge(sm["sAVE"], rT2)
                            tensor.reg_add(rWAT, rWAT, 16)
                            vtile = [0]

                            def vblock(nt, tensor=tensor, po=po, vtile=vtile):
                                for t in range(vtile[0], vtile[0] + nt):
                                    tensor.wait_ge(sm["watv"], rWAT)
                                    if t >= 2:
                                        tensor.reg_alu(rT1, rPVB, t - 1, ALU.add)
                                        tensor.wait_ge(sm["sPV"], rT1)
                                    vp = 2048 + (t % 2) * 512
                                    for kcj in range(KC):
                                        ins = tensor.matmul(
                                            psum[:, vp:vp + 128],
                                            xbf[:, kcj * T + t * 128:kcj * T + t * 128 + 128],
                                            wv_p[:, po + kcj * 128:po + kcj * 128 + 128],
                                            start=(kcj == 0), stop=(kcj == KC - 1),
                                            skip_group_check=True)
                                    ins.then_inc(sm["sPJ"], 1)
                                vtile[0] += nt

                            for wi, (wsn, wt) in enumerate((("watq", wq_p), ("watk", wk_p), ("watkp", wkp_p))):
                                tensor.wait_ge(sm[wsn], rWAT)
                                if wi >= 1:
                                    tensor.reg_add(rPEW, rPEW, 1)
                                    tensor.wait_ge(sm["sPE"], rPEW)
                                for kcj in range(KC):
                                    for nb in range(4):
                                        ins = tensor.matmul(
                                            psum[:, nb * 512:nb * 512 + 512],
                                            wt[:, po + kcj * 128:po + kcj * 128 + 128],
                                            xbf[:, kcj * T + nb * 512:kcj * T + nb * 512 + 512],
                                            start=(kcj == 0), stop=(kcj == KC - 1),
                                            skip_group_check=True)
                                ins.then_inc(sm["sPJ"], 1)
                                vblock(4 if wi < 2 else 8)

                            tensor.reg_add(rPEW, rPEW, 1)
                            tensor.wait_ge(sm["sPE"], rPEW)
                            tensor.reg_alu(rT1, rPVB, 16, ALU.add)
                            tensor.wait_ge(sm["sPV"], rT1)
                            for qh in range(4):
                                tensor.reg_add(rRC, rRC, 1)
                                if hpar == 0 and qh == 0:
                                    tensor.reg_alu(rT2, hpi, 1, ALU.mult)
                                    guard = tensor.If_ne(rT2, 0)
                                    guard.__enter__()
                                tensor.reg_alu(rT2, rRC, 1, ALU.subtract)
                                tensor.wait_ge(sm["sRC"], rT2)
                                tensor.reg_alu(rT1, rT2, 1, ALU.subtract)
                                tensor.wait_ge(sm["sBL"], rT1)
                                for mi in range(2):
                                    bco = BCAST + mi * 512
                                    ins = tensor.matmul(
                                        psum[0:64, bco:bco + 512],
                                        onesf[64:65, 0:64], rcp_sb[64:65, mi * 512:mi * 512 + 512],
                                        start=True, stop=True, skip_group_check=True)
                                    ins = tensor.matmul(
                                        psum[64:128, bco:bco + 512],
                                        onesf[0:1, 0:64], rcp_sb[0:1, mi * 512:mi * 512 + 512],
                                        start=True, stop=True, skip_group_check=True)
                                ins.then_inc(sm["sBC"], 1)
                                if hpar == 0 and qh == 0:
                                    guard.__exit__(None, None, None)
                                tensor.reg_add(rAVE, rAVE, 2)
                                tensor.wait_ge(sm["sAVE"], rAVE)
                                tensor.reg_alu(rEX, rKCB, qh * 32, ALU.add)

                                def st_of(j, tensor=tensor, qh=qh):
                                    mat, kc = j // 16, j % 16
                                    kf = kp_fm if mat else k_fm
                                    stb = 0 if kc % 2 == 0 else (2048 if mat == 0 else 1024)
                                    ins = tensor.matmul(
                                        psum[:, stb:stb + 512],
                                        kf[0:64, kc * 128:kc * 128 + 128],
                                        q_fm[0:64, qh * QW:qh * QW + QW],
                                        start=True, stop=True, tile_position=(0, 0),
                                        skip_group_check=True)
                                    ins = tensor.matmul(
                                        psum[:, stb + 512:stb + 1024],
                                        kf[64:128, kc * 128:kc * 128 + 128],
                                        q_fm[64:128, qh * QW:qh * QW + QW],
                                        start=True, stop=True, tile_position=(64, 0),
                                        skip_group_check=True)
                                    ins.then_inc(sm["sST"], 1)

                                def av_of(j, tensor=tensor):
                                    mat, kc = j // 16, j % 16
                                    avo = AV_P if mat else AV_STD
                                    slot = (kc % 2) * 1024
                                    ins = tensor.matmul(
                                        psum[0:65, avo:avo + 512],
                                        v_p[:, kc * 130:kc * 130 + 65],
                                        attnT[:, slot:slot + 512],
                                        start=(kc == 0), stop=(kc == 15),
                                        skip_group_check=True)
                                    ins = tensor.matmul(
                                        psum[0:64, avo + 512:avo + 1024],
                                        ones64[:],
                                        attnT[:, slot + 512:slot + 1024],
                                        start=(kc == 0), stop=(kc == 15),
                                        skip_group_check=True)
                                    ins = tensor.matmul(
                                        psum[64:128, avo + 512:avo + 1024],
                                        v_p[:, kc * 130 + 65:kc * 130 + 129],
                                        attnT[:, slot + 512:slot + 1024],
                                        start=(kc == 0), stop=(kc == 15),
                                        skip_group_check=True)
                                    ins.then_inc(sm["sAV"], 1)

                                for j in range(32):
                                    if j == 17:
                                        tensor.reg_alu(rT1, rAVE, 1, ALU.add)
                                        tensor.wait_ge(sm["sAVE"], rT1)
                                    tensor.wait_ge(sm["sEX"], rEX)
                                    st_of(j)
                                    tensor.reg_add(rEX, rEX, 1)
                                    if j >= 1:
                                        tensor.wait_ge(sm["sEX"], rEX)
                                        av_of(j - 1)
                                tensor.reg_add(rEX, rEX, 1)
                                tensor.wait_ge(sm["sEX"], rEX)
                                av_of(31)
                            tensor.reg_add(rPVB, rPVB, 16)
                            tensor.reg_add(rKCB, rKCB, 128)
                    tensor.wait_ge(sm["sRC"], 32)
                    tensor.wait_ge(sm["sBL"], 31)
                    for mi in range(2):
                        bco = BCAST + mi * 512
                        ins = tensor.matmul(psum[0:64, bco:bco + 512],
                                            onesf[64:65, 0:64], rcp_sb[64:65, mi * 512:mi * 512 + 512],
                                            start=True, stop=True, skip_group_check=True)
                        ins = tensor.matmul(psum[64:128, bco:bco + 512],
                                            onesf[0:1, 0:64], rcp_sb[0:1, mi * 512:mi * 512 + 512],
                                            start=True, stop=True, skip_group_check=True)
                    ins.then_inc(sm["sBC"], 1)

                @blk.scalar
                def _(scalar):
                    rW = scalar.alloc_register("rW")
                    rKC = scalar.alloc_register("rKC")
                    scalar.reg_mov(rKC, 1)
                    with scalar.Fori(0, HP // 2) as hpi:
                        for hpar in range(2):
                            with scalar.Fori(0, 4) as qh:
                                scalar.reg_alu(rW, qh, 32, ALU.mult)
                                scalar.reg_alu(rW, rW, rKC, ALU.add)
                                for mat in range(2):
                                    for kc in range(16):
                                        stb = 0 if kc % 2 == 0 else (2048 if mat == 0 else 1024)
                                        scalar.wait_ge(sm["sST"], rW)
                                        scalar.wait_ge(sm["sAV"], rW)
                                        scalar.activation(
                                            attnT[:, (kc % 2) * 1024:(kc % 2) * 1024 + 1024],
                                            psum[:, stb:stb + 1024],
                                            AF.Exp).then_inc(sm["sEX"], 1)
                                        scalar.reg_add(rW, rW, 1)
                            scalar.reg_add(rKC, rKC, 128)

                @blk.vector
                def _(vector):
                    vector.memset(
                        v_p[:, :].rearrange("p (t x c) -> p t x c", t=16, x=2)[:, :, :, 64:65], 1.0)

                    def blend_v2(bq, vector=vector):
                        co = bq * QW
                        vector.tensor_copy(rb[:], psum[:, BCAST:BCAST + 1024])
                        vector.drain()
                        vector.tensor_mul(av_sb[0:64, 0:512], av_sb[0:64, 0:512], rb[0:64, 0:512])
                        vector.tensor_mul(av_sb[64:128, 512:1024], av_sb[64:128, 512:1024], rb[64:128, 0:512])
                        vector.tensor_mul(av_sb[0:64, 1024:1536], av_sb[0:64, 1024:1536], rb[0:64, 512:1024])
                        vector.tensor_mul(av_sb[64:128, 1536:2048], av_sb[64:128, 1536:2048], rb[64:128, 512:1024])
                        vector.drain()
                        vector.tensor_add(attn_out[0:64, co:co + QW],
                                          av_sb[0:64, 0:512], av_sb[0:64, 1024:1536])
                        vector.tensor_add(attn_out[64:128, co:co + QW],
                                          av_sb[64:128, 512:1024], av_sb[64:128, 1536:2048]
                                          ).then_inc(sm["sBL"], 1)
                        vector.drain()

                    for hp in range(HP):
                        PJ = hp * 19
                        if hp >= 1:
                            vector.wait_ge(sm["sST"], 128 * hp)
                            vector.wait_ge(sm["sAV"], 128 * hp + SEED["sAV"])

                        def vevac(t, hp=hp, vector=vector):
                            gidx = hp * 19 + (2 + t if t < 4 else (3 + t if t < 8 else 4 + t))
                            vector.wait_ge(sm["sPJ"], gidx)
                            vp = 2048 + (t % 2) * 512
                            vector.tensor_copy(
                                v_p[:, t * 130:t * 130 + 130].rearrange(
                                    "p (h d) -> p h d", h=2)[:, :, 0:64],
                                psum[:, vp:vp + 128].rearrange("p (h d) -> p h d", h=2),
                            ).then_inc(sm["sPV"], 1)

                        vector.wait_ge(sm["sPJ"], PJ + 1)
                        vector.tensor_scalar_mul(q_fm[:], psum[:, 0:2048], 0.125).then_inc(sm["sPE"], 1)
                        for t in range(4):
                            vevac(t)
                        vector.wait_ge(sm["sPJ"], PJ + 6)
                        vector.tensor_copy(k_fm[:], psum[:, 0:2048]).then_inc(sm["sPE"], 1)
                        for t in range(4, 8):
                            vevac(t)
                        vector.wait_ge(sm["sPJ"], PJ + 11)
                        vector.tensor_copy(kp_fm[:], psum[:, 0:2048]).then_inc(sm["sPE"], 1)
                        for t in range(8, 16):
                            vevac(t)
                        for qh in range(4):
                            bq = hp * 4 + qh
                            if bq >= 1:
                                vector.wait_ge(sm["sBC"], bq)
                                blend_v2(bq - 1)
                            vector.wait_ge(sm["sAV"], 16 * (2 * bq + 1) + SEED["sAV"])
                            vector.tensor_copy(av_sb[:, 0:1024],
                                               psum[:, AV_STD:AV_STD + 1024]).then_inc(sm["sAVE"], 1)
                            vector.wait_ge(sm["sAV"], 16 * (2 * bq + 2) + SEED["sAV"])
                            vector.tensor_copy(av_sb[:, 1024:2048],
                                               psum[:, AV_P:AV_P + 1024]).then_inc(sm["sAVE"], 1)
                            vector.drain()
                            vector.reciprocal(
                                rcp_sb[64:65, :].rearrange("p (m w) -> p m w", m=2, w=512),
                                av_sb[64:65, :].rearrange("p (m w) -> p m w", m=2, w=1024)[:, :, 0:512])
                            vector.reciprocal(
                                rcp_sb[0:1, :].rearrange("p (m w) -> p m w", m=2, w=512),
                                av_sb[0:1, :].rearrange("p (m w) -> p m w", m=2, w=1024)[:, :, 512:1024])
                            vector.drain()
                            vector.tensor_scalar_mul(rcp_sb[64:65, 0:512], rcp_sb[64:65, 0:512],
                                                     alpha_a[64:65, 0:1])
                            vector.tensor_scalar_mul(rcp_sb[0:1, 0:512], rcp_sb[0:1, 0:512],
                                                     alpha_a[0:1, 0:1])
                            vector.tensor_scalar_mul(rcp_sb[64:65, 512:1024], rcp_sb[64:65, 512:1024],
                                                     alpha_a[64:65, 1:2])
                            vector.tensor_scalar_mul(rcp_sb[0:1, 512:1024], rcp_sb[0:1, 512:1024],
                                                     alpha_a[0:1, 1:2]).then_inc(sm["sRC"], 1)
                    vector.wait_ge(sm["sBC"], 32)
                    blend_v2(31)

            if stage >= 2:
                @blk.tensor
                def _(tensor):
                    tensor.wait_ge(sm["sBL"], 32)
                    for mc in range(KC):
                        tensor.wait_ge(sm["wol%d" % (mc % 2)], 16 * (mc // 2 + 1))
                        if mc >= 2:
                            tensor.wait_ge(sm["sWOE"], mc - 1)
                        pc = (mc % 2) * 2048
                        for kcj in range(KC):
                            for nb in range(4):
                                ins = tensor.matmul(
                                    psum[:, pc + nb * 512:pc + nb * 512 + 512],
                                    wo_p[:, (mc % 2) * 1024 + kcj * 128:(mc % 2) * 1024 + kcj * 128 + 128],
                                    attn_out[:, kcj * T + nb * 512:kcj * T + nb * 512 + 512],
                                    start=(kcj == 0), stop=(kcj == KC - 1),
                                    skip_group_check=True)
                        ins.then_inc(sm["sWO"], 1)

                @blk.vector
                def _(vector):
                    for mc in range(KC):
                        vector.wait_ge(sm["sWO"], mc + 1)
                        pc = (mc % 2) * 2048
                        vector.tensor_add(xf32[:, mc * T:mc * T + T],
                                          xf32[:, mc * T:mc * T + T],
                                          psum[:, pc:pc + 2048]).then_inc(sm["sWOE"], 1)

            if stage >= 3:
                @blk.tensor
                def _(tensor):
                    tensor.wait_ge(sm["sWOE"], KC)
                    for tc in range(TC):
                        if tc >= 2:
                            tensor.wait_ge(sm["sL1C"], tc - 1)
                        pc = (tc % 2) * 1024
                        for cc in range(KC):
                            ins = tensor.transpose(
                                psum[:, pc + cc * 128:pc + cc * 128 + 128],
                                xf32[:, cc * T + tc * 128:cc * T + tc * 128 + 128], idf[:])
                        ins.then_inc(sm["sL1T"], 1)
                        tensor.wait_ge(sm["sL1N"], tc + 1)
                        if tc >= 2:
                            tensor.wait_ge(sm["sL1E"], tc - 1)
                        pb = 2048 + (tc % 2) * 1024
                        for cc in range(KC):
                            ins = tensor.transpose(
                                psum[:, pb + cc * 128:pb + cc * 128 + 128],
                                ysq[:, (tc % 2) * 1024 + cc * 128:(tc % 2) * 1024 + cc * 128 + 128], idf[:])
                        ins.then_inc(sm["sL1B"], 1)

                @blk.scalar
                def _(scalar):
                    for tc in range(TC):
                        scalar.wait_ge(sm["sLNa"], tc + 1)
                        scalar.activation(stat[:, 8:9], stat[:, 7:8], AF.Sqrt).then_inc(sm["sLNb"], 1)

                @blk.vector
                def _(vector):
                    for tc in range(TC):
                        sl = (tc % 2) * 1024
                        vector.wait_ge(sm["sL1T"], tc + 1)
                        if tc >= 2:
                            vector.wait_ge(sm["sL1B"], tc - 1)
                        vector.tensor_copy(ytm[:, sl:sl + 1024], psum[:, sl:sl + 1024]).then_inc(sm["sL1C"], 1)
                        vector.drain()
                        vector.reduce_sum(stat[:, 0:1], ytm[:, sl:sl + 1024], axis=AX.X)
                        vector.tensor_mul(ysq[:, sl:sl + 1024], ytm[:, sl:sl + 1024], ytm[:, sl:sl + 1024])
                        vector.drain()
                        vector.reduce_sum(stat[:, 1:2], ysq[:, sl:sl + 1024], axis=AX.X)
                        vector.drain()
                        vector.tensor_scalar_mul(stat[:, 2:3], stat[:, 0:1], 1.0 / C)
                        vector.tensor_scalar_mul(stat[:, 3:4], stat[:, 1:2], 1.0 / C)
                        vector.drain()
                        vector.tensor_mul(stat[:, 4:5], stat[:, 2:3], stat[:, 2:3])
                        vector.drain()
                        vector.tensor_sub(stat[:, 5:6], stat[:, 3:4], stat[:, 4:5])
                        vector.drain()
                        vector.tensor_scalar_add(stat[:, 6:7], stat[:, 5:6], 1e-5)
                        vector.drain()
                        vector.reciprocal(stat[:, 7:8], stat[:, 6:7]).then_inc(sm["sLNa"], 1)
                        vector.wait_ge(sm["sLNb"], tc + 1)
                        vector.drain()
                        vector.tensor_scalar(ysq[:, sl:sl + 1024], ytm[:, sl:sl + 1024],
                                             stat[:, 2:3], stat[:, 8:9],
                                             ALU.subtract, ALU.mult).then_inc(sm["sL1N"], 1)
                        vector.wait_ge(sm["sL1B"], tc + 1)
                        pb = 2048 + (tc % 2) * 1024
                        pin = psum[:, pb:pb + 1024].rearrange("p (cc t) -> p cc t", cc=KC)
                        vector.tensor_copy(
                            xf32[:, :].rearrange("p (cc t) -> p cc t", cc=KC)[:, :, tc * 128:tc * 128 + 128],
                            pin)
                        vector.tensor_copy(
                            xbf[:, :].rearrange("p (cc t) -> p cc t", cc=KC)[:, :, tc * 128:tc * 128 + 128],
                            pin).then_inc(sm["sL1E"], 1)
                        vector.drain()

            @blk.sync
            def _(sync):
                tap(sync, "xf32", xf32[:], "sXE", TC)
                tap(sync, "qfm", q_fm[:], "sPE", 22)
                tap(sync, "kfm", k_fm[:], "sPE", 23)
                tap(sync, "vp", v_p[:], "sPV", 128)
                tap(sync, "attn_out", attn_out[:], "sBL", 32)
                tap(sync, "x2", xf32[:], "sL1E", TC)
                tap(sync, "x2bf", xbf[:], "sL1E", TC)

        fctx = contextlib.ExitStack()
        with fctx:
            w1g = fctx.enter_context(nc.sbuf_tensor("w1g", [128, 2 * KC * FG], BF16))
            w3g = fctx.enter_context(nc.sbuf_tensor("w3g", [128, 2 * KC * FG], BF16))
            w2g = fctx.enter_context(nc.sbuf_tensor("w2g", [128, 2 * FCG * C], BF16))
            wgb = fctx.enter_context(nc.sbuf_tensor("wgb", [128, KC * 8], BF16))
            wbc_all = fctx.enter_context(nc.sbuf_tensor("wbc_all", [128, E * T], BF16))
            wbc = fctx.enter_context(nc.sbuf_tensor("wbc", [128, T], BF16))
            h_w = fctx.enter_context(nc.sbuf_tensor("h_w", [128, 2 * FCG * TTW], BF16))
            s1 = fctx.enter_context(nc.sbuf_tensor("s1", [128, 2 * TTW], BF16))
            lg_sb = ysq
            combT = ysq
            et_sb = ytm[:, 0:128]
            comb_tm = ytm[:, 128:256]
            rwk = ytm[:, 256:304]

            if stage >= 4:
                @blk.gpsimd
                def _(g):
                    g.wait_ge(sm["sBL"], 32)
                    g.dma_start(out=wgb[:].rearrange("p (kc e) -> p kc e", kc=KC),
                                in_=wg_d.rearrange("(kc p) e -> p kc e", p=128)).then_inc(sm["wgl"], 16)

                @blk.tensor
                def _(tensor):
                    tensor.wait_ge(sm["sL1E"], TC)
                    tensor.wait_ge(sm["wgl"], 16)
                    for nb in range(4):
                        for kcj in range(KC):
                            ins = tensor.matmul(
                                psum[0:8, nb * 512:nb * 512 + 512],
                                wgb[:, kcj * 8:kcj * 8 + 8],
                                xbf[:, kcj * T + nb * 512:kcj * T + nb * 512 + 512],
                                start=(kcj == 0), stop=(kcj == KC - 1),
                                skip_group_check=True)
                    ins.then_inc(sm["sRT"], 1)
                    tensor.wait_ge(sm["sRTE"], 1)
                    for c in range(16):
                        ins = tensor.transpose(psum[:, 2048 + c * 8:2048 + c * 8 + 8],
                                               lg_sb[0:8, c * 128:c * 128 + 128], idf[0:8, 0:8])
                    ins.then_inc(sm["sRT"], 1)
                    tensor.wait_ge(sm["sRTT"], 1)
                    for c in range(16):
                        ins = tensor.transpose(psum[0:8, c * 128:c * 128 + 128],
                                               comb_tm[:, c * 8:c * 8 + 8], idf[:])
                    ins.then_inc(sm["sRT"], 1)

                @blk.scalar
                def _(scalar):
                    scalar.wait_ge(sm["sRT"], 2)
                    scalar.activation(et_sb[:], psum[:, 2048:2176], AF.Exp).then_inc(sm["sREX"], 1)

                @blk.vector
                def _(vector):
                    vector.wait_ge(sm["sRT"], 1)
                    vector.tensor_copy(lg_sb[0:8, 0:2048], psum[0:8, 0:2048]).then_inc(sm["sRTE"], 1)
                    vector.wait_ge(sm["sREX"], 1)
                    for c in range(16):
                        ec = et_sb[:, c * 8:c * 8 + 8]
                        vector.reduce_max(rwk[:, 0:1], ec, axis=AX.X)
                        vector.reduce_sum(rwk[:, 4:5], ec, axis=AX.X)
                        vector.drain()
                        vector.tensor_scalar(rwk[:, 8:16], ec, rwk[:, 0:1], None, ALU.is_equal)
                        vector.drain()
                        vector.scalar_tensor_tensor(rwk[:, 16:24], rwk[:, 8:16], -1000.0, iotaP[:],
                                                    ALU.mult, ALU.add)
                        vector.drain()
                        vector.tensor_reduce(rwk[:, 1:2], rwk[:, 16:24], axis=AX.X, op=ALU.min)
                        vector.drain()
                        vector.tensor_scalar(rwk[:, 8:16], iota8[:], rwk[:, 1:2], None, ALU.is_equal)
                        vector.drain()
                        vector.scalar_tensor_tensor(rwk[:, 16:24], rwk[:, 8:16], -1e9, ec,
                                                    ALU.mult, ALU.add)
                        vector.drain()
                        vector.reduce_max(rwk[:, 2:3], rwk[:, 16:24], axis=AX.X)
                        vector.drain()
                        vector.tensor_scalar(rwk[:, 24:32], rwk[:, 16:24], rwk[:, 2:3], None, ALU.is_equal)
                        vector.drain()
                        vector.scalar_tensor_tensor(rwk[:, 32:40], rwk[:, 24:32], -1000.0, iotaP[:],
                                                    ALU.mult, ALU.add)
                        vector.drain()
                        vector.tensor_reduce(rwk[:, 3:4], rwk[:, 32:40], axis=AX.X, op=ALU.min)
                        vector.drain()
                        vector.tensor_scalar(rwk[:, 24:32], iota8[:], rwk[:, 3:4], None, ALU.is_equal)
                        vector.tensor_add(rwk[:, 5:6], rwk[:, 0:1], rwk[:, 2:3])
                        vector.drain()
                        vector.scalar_tensor_tensor(rwk[:, 6:7], rwk[:, 4:5], 1e-8, rwk[:, 5:6],
                                                    ALU.mult, ALU.add)
                        vector.drain()
                        vector.reciprocal(rwk[:, 7:8], rwk[:, 6:7])
                        vector.drain()
                        vector.tensor_mul(rwk[:, 40:41], rwk[:, 0:1], rwk[:, 7:8])
                        vector.tensor_mul(rwk[:, 41:42], rwk[:, 2:3], rwk[:, 7:8])
                        vector.drain()
                        vector.tensor_scalar(rwk[:, 32:40], rwk[:, 8:16], rwk[:, 40:41], None, ALU.mult)
                        vector.drain()
                        vector.scalar_tensor_tensor(comb_tm[:, c * 8:c * 8 + 8],
                                                    rwk[:, 24:32], rwk[:, 41:42], rwk[:, 32:40],
                                                    ALU.mult, ALU.add)
                        vector.drain()
                    vector.nop().then_inc(sm["sRTT"], 1)
                    vector.wait_ge(sm["sRT"], 3)
                    vector.tensor_copy(combT[0:8, 0:2048], psum[0:8, 0:2048]).then_inc(sm["sRTC"], 1)

            if stage >= 5:
                @blk.gpsimd
                def _(g):
                    for n in range(E * G):
                        e, gg = n // G, n % G
                        par = n % 2
                        if gg == 0:
                            g.wait_ge(sm["sWBE"], e + 1)
                            if e >= 1:
                                g.wait_ge(sm["sHw"], 256 * e + SEED["sHw"])
                            g.dma_start(out=wbc[:], in_=wbc_all[:, e * T:(e + 1) * T]
                                        ).then_inc(sm["sWC"], 16)
                        if n < 2:
                            g.wait_ge(sm["sBL"], 32)
                        elif n % 2 == 0:
                            g.wait_ge(sm["sEv"], max(0, 16 * (n // 2 - 1) + 8))
                        else:
                            g.wait_ge(sm["sEv"], 16 * (n // 2))
                        g.dma_start(
                            out=w1g[:, par * KC * FG:(par + 1) * KC * FG].rearrange(
                                "p (kc f) -> p kc f", kc=KC),
                            in_=w1_d[e].rearrange("(kc p) f -> p kc f", p=128)[:, :, gg * FG:(gg + 1) * FG],
                        ).then_inc(sm["wf%d" % par], 16)
                        g.dma_start(
                            out=w3g[:, par * KC * FG:(par + 1) * KC * FG].rearrange(
                                "p (kc f) -> p kc f", kc=KC),
                            in_=w3_d[e].rearrange("(kc p) f -> p kc f", p=128)[:, :, gg * FG:(gg + 1) * FG],
                        ).then_inc(sm["wf%d" % par], 16)
                        g.dma_start(
                            out=w2g[:, par * FCG * C:(par + 1) * FCG * C].rearrange(
                                "p (fc c) -> p fc c", fc=FCG),
                            in_=w2_d[e, gg * FG:(gg + 1) * FG, :].rearrange("(fc p) c -> p fc c", p=128),
                        ).then_inc(sm["wf%d" % par], 16)

                @blk.tensor
                def _(tensor):
                    tensor.wait_ge(sm["sRTC"], 1)
                    for e in range(E):
                        if e >= 1:
                            tensor.wait_ge(sm["sWBE"], e)
                        for nb in range(4):
                            ins = tensor.matmul(psum[:, nb * 512:nb * 512 + 512],
                                                selb[:, e * 128:e * 128 + 128],
                                                combT[0:8, nb * 512:nb * 512 + 512],
                                                start=True, stop=True, skip_group_check=True)
                        ins.then_inc(sm["sWB"], 1)
                    tensor.wait_ge(sm["sWBE"], E)
                    rWF = tensor.alloc_register("rWF")
                    rSL = tensor.alloc_register("rSL")
                    rHW = tensor.alloc_register("rHW")
                    rEV = tensor.alloc_register("rEV")
                    rTF = tensor.alloc_register("rTF")
                    tensor.reg_mov(rWF, 0)
                    tensor.reg_mov(rSL, 0)
                    tensor.reg_mov(rHW, 0)
                    tensor.reg_mov(rEV, 0)
                    with tensor.Fori(0, E * G // 2) as p:
                        tensor.reg_add(rWF, rWF, 48)
                        for par in range(2):
                            tensor.wait_ge(sm["wf%d" % par], rWF)
                            for tt in range(NTT):
                                tto = tt * TTW
                                wo1 = par * KC * FG
                                wo2 = par * FCG * C
                                hwo = (tt % 2) * FCG * TTW
                                for fc in range(FCG):
                                    hA = HPS + (fc % 2) * 1024
                                    hB = hA + 512
                                    tensor.reg_add(rSL, rSL, 1)
                                    tensor.wait_ge(sm["sSl"], rSL)
                                    for kcj in range(KC):
                                        ins = tensor.matmul(
                                            psum[:, hA:hA + TTW],
                                            w1g[:, wo1 + kcj * FG + fc * 128:wo1 + kcj * FG + fc * 128 + 128],
                                            xbf[:, kcj * T + tto:kcj * T + tto + TTW],
                                            start=(kcj == 0), stop=(kcj == KC - 1),
                                            skip_group_check=True)
                                    ins.then_inc(sm["sA"], 1)
                                    tensor.reg_add(rHW, rHW, 1)
                                    tensor.wait_ge(sm["sHw"], rHW)
                                    for kcj in range(KC):
                                        ins = tensor.matmul(
                                            psum[:, hB:hB + TTW],
                                            w3g[:, wo1 + kcj * FG + fc * 128:wo1 + kcj * FG + fc * 128 + 128],
                                            xbf[:, kcj * T + tto:kcj * T + tto + TTW],
                                            start=(kcj == 0), stop=(kcj == KC - 1),
                                            skip_group_check=True)
                                    ins.then_inc(sm["sB"], 1)
                                tensor.reg_alu(rTF, rHW, 2, ALU.add)
                                tensor.wait_ge(sm["sHw"], rTF)
                                tensor.wait_ge(sm["sEv"], rEV)
                                for fc in range(FCG):
                                    for m in range(KC):
                                        ins = tensor.matmul(
                                            psum[:, m * TTW:m * TTW + TTW],
                                            w2g[:, wo2 + fc * C + m * 128:wo2 + fc * C + m * 128 + 128],
                                            h_w[:, hwo + fc * TTW:hwo + fc * TTW + TTW],
                                            start=(fc == 0 and m % 2 == 0), stop=(fc == FCG - 1),
                                            skip_group_check=True)
                                ins.then_inc(sm["sW2"], 1)
                                tensor.reg_add(rEV, rEV, 1)

                @blk.scalar
                def _(scalar):
                    rSA = scalar.alloc_register("rSA")
                    rHW = scalar.alloc_register("rHW2")
                    scalar.reg_mov(rSA, 0)
                    scalar.reg_mov(rHW, 0)
                    with scalar.Fori(0, E * G // 2) as p:
                        for par in range(2):
                            for tt in range(NTT):
                                for fc in range(FCG):
                                    scalar.reg_add(rSA, rSA, 1)
                                    scalar.wait_ge(sm["sA"], rSA)
                                    scalar.reg_add(rHW, rHW, 1)
                                    scalar.wait_ge(sm["sHw"], rHW)
                                    hA = HPS + (fc % 2) * 1024
                                    scalar.activation(s1[:, (fc % 2) * TTW:(fc % 2) * TTW + TTW],
                                                      psum[:, hA:hA + TTW],
                                                      AF.Silu).then_inc(sm["sSl"], 1)

                @blk.vector
                def _(vector):
                    for e in range(E):
                        vector.wait_ge(sm["sWB"], e + 1)
                        vector.tensor_copy(wbc_all[:, e * T:(e + 1) * T],
                                           psum[:, 0:2048]).then_inc(sm["sWBE"], 1)
                    rB = vector.alloc_register("fB")
                    rSl = vector.alloc_register("fSl")
                    rW2a = vector.alloc_register("fW2a")
                    rW2b = vector.alloc_register("fW2b")
                    rWC = vector.alloc_register("fWC")
                    rVT = vector.alloc_register("fVT")
                    vector.reg_mov(rB, 0)
                    vector.reg_mov(rSl, 2)
                    vector.reg_mov(rW2a, 0)
                    vector.reg_mov(rW2b, 2)
                    vector.reg_mov(rWC, 0)
                    with vector.Fori(0, E * G // 2) as p:
                        vector.reg_alu(rVT, p, 4, ALU.mod)
                        with vector.If_eq(rVT, 0):
                            vector.reg_add(rWC, rWC, 16)
                        vector.wait_ge(sm["sWC"], rWC)
                        for par in range(2):
                            for tt in range(NTT):
                                tto = tt * TTW
                                hwo = (tt % 2) * FCG * TTW
                                vector.wait_ge(sm["sW2"], rW2a)
                                vector.reg_add(rW2a, rW2a, 1)
                                for fc in range(FCG):
                                    hB = HPS + (fc % 2) * 1024 + 512
                                    vector.reg_add(rB, rB, 1)
                                    vector.wait_ge(sm["sB"], rB)
                                    vector.tensor_mul(psum[:, hB:hB + TTW], psum[:, hB:hB + TTW],
                                                      wbc[:, tto:tto + TTW])
                                    vector.drain()
                                    vector.reg_add(rSl, rSl, 1)
                                    vector.wait_ge(sm["sSl"], rSl)
                                    vector.tensor_mul(
                                        h_w[:, hwo + fc * TTW:hwo + fc * TTW + TTW],
                                        s1[:, (fc % 2) * TTW:(fc % 2) * TTW + TTW],
                                        psum[:, hB:hB + TTW]).then_inc(sm["sHw"], 1)
                                vector.wait_ge(sm["sW2"], rW2b)
                                vector.reg_add(rW2b, rW2b, 1)
                                vector.tensor_add(
                                    xf32[:, :].rearrange("p (cc t) -> p cc t", cc=KC)[:, :, tto:tto + TTW],
                                    xf32[:, :].rearrange("p (cc t) -> p cc t", cc=KC)[:, :, tto:tto + TTW],
                                    psum[:, 0:2048].rearrange("p (m t) -> p m t", m=KC),
                                ).then_inc(sm["sEv"], 1)

            if stage >= 6:
                @blk.tensor
                def _(tensor):
                    tensor.wait_ge(sm["sEv"], 512)
                    for tc in range(TC):
                        if tc >= 2:
                            tensor.wait_ge(sm["sFC"], tc - 1)
                        pc = (tc % 2) * 1024
                        for cc in range(KC):
                            ins = tensor.transpose(
                                psum[:, pc + cc * 128:pc + cc * 128 + 128],
                                xf32[:, cc * T + tc * 128:cc * T + tc * 128 + 128], idf[:])
                        ins.then_inc(sm["sFT"], 1)

                @blk.scalar
                def _(scalar):
                    for tc in range(TC):
                        scalar.wait_ge(sm["sLNa"], TC + tc + 1)
                        scalar.activation(stat[:, 8:9], stat[:, 7:8], AF.Sqrt).then_inc(sm["sLNb"], 1)

                @blk.vector
                def _(vector):
                    for tc in range(TC):
                        sl = (tc % 2) * 1024
                        vector.wait_ge(sm["sFT"], tc + 1)
                        vector.tensor_copy(ytm[:, sl:sl + 1024], psum[:, sl:sl + 1024]).then_inc(sm["sFC"], 1)
                        vector.drain()
                        vector.reduce_sum(stat[:, 0:1], ytm[:, sl:sl + 1024], axis=AX.X)
                        if tc >= 2:
                            vector.wait_ge(sm["sty%d" % (tc % 2)], 16 * (tc // 2))
                        vector.tensor_mul(ysq[:, sl:sl + 1024], ytm[:, sl:sl + 1024], ytm[:, sl:sl + 1024])
                        vector.drain()
                        vector.reduce_sum(stat[:, 1:2], ysq[:, sl:sl + 1024], axis=AX.X)
                        vector.drain()
                        vector.tensor_scalar_mul(stat[:, 2:3], stat[:, 0:1], 1.0 / C)
                        vector.tensor_scalar_mul(stat[:, 3:4], stat[:, 1:2], 1.0 / C)
                        vector.drain()
                        vector.tensor_mul(stat[:, 4:5], stat[:, 2:3], stat[:, 2:3])
                        vector.drain()
                        vector.tensor_sub(stat[:, 5:6], stat[:, 3:4], stat[:, 4:5])
                        vector.drain()
                        vector.tensor_scalar_add(stat[:, 6:7], stat[:, 5:6], 1e-5)
                        vector.drain()
                        vector.reciprocal(stat[:, 7:8], stat[:, 6:7]).then_inc(sm["sLNa"], 1)
                        vector.wait_ge(sm["sLNb"], TC + tc + 1)
                        vector.drain()
                        vector.tensor_scalar(ysq[:, sl:sl + 1024], ytm[:, sl:sl + 1024],
                                             stat[:, 2:3], stat[:, 8:9],
                                             ALU.subtract, ALU.mult).then_inc(sm["sFN"], 1)
                        vector.drain()

                @blk.sync
                def _(sync):
                    for tc in range(TC):
                        sync.wait_ge(sm["sFN"], tc + 1)
                        sync.dma_start(out=y_d[tc * 128:(tc + 1) * 128, :],
                                       in_=ysq[:, (tc % 2) * 1024:(tc % 2) * 1024 + 1024]
                                       ).then_inc(sm["sty%d" % (tc % 2)], 16)

            @blk.sync
            def _(sync):
                tap(sync, "combT", combT[0:8, 0:T], "sRTC", 1)
                tap(sync, "acc", xf32[:], "sEv", 512)
                if n_stores[0]:
                    sync.wait_ge(sm["st"], 16 * n_stores[0])
                if stage >= 6:
                    sync.wait_ge(sm["sty0"], 16 * TC // 2)
                    sync.wait_ge(sm["sty1"], 16 * TC // 2)

    return nc, tap_d


def _prep_inputs(inputs):
    f32 = np.float32
    base = {
        "wqT": np.ascontiguousarray(np.asarray(inputs["L0_attn_W_Q"], dtype=f32).T),
        "wkT": np.ascontiguousarray(np.asarray(inputs["L0_attn_W_K"], dtype=f32).T),
        "wkpT": np.ascontiguousarray(np.asarray(inputs["L0_attn_W_K_prime"], dtype=f32).T),
        "wvT": np.ascontiguousarray(np.asarray(inputs["L0_attn_W_V"], dtype=f32).T),
        "woT": np.ascontiguousarray(np.asarray(inputs["L0_attn_W_O"], dtype=f32).T),
        "alpha": np.asarray(inputs["L0_attn_alpha"], dtype=f32).reshape(1, 1),
        "wg": np.ascontiguousarray(np.asarray(inputs["L0_ffn_Wg"], dtype=f32)),
        "w1": np.ascontiguousarray(np.asarray(inputs["L0_ffn_W1"], dtype=f32)),
        "w2": np.ascontiguousarray(np.asarray(inputs["L0_ffn_W2"], dtype=f32)),
        "w3": np.ascontiguousarray(np.asarray(inputs["L0_ffn_W3"], dtype=f32)),
    }
    x = np.asarray(inputs["L0_attn_x"], dtype=f32)
    in_maps = []
    for b in range(8):
        m = dict(base)
        m["x"] = np.ascontiguousarray(x[b])
        in_maps.append(m)
    return in_maps


def kernel(**inputs):
    nc = bass.Bass("TRN2", target_bir_lowering=False)
    build(nc)
    in_maps = _prep_inputs(inputs)
    res = run_bass_kernel_spmd(nc, in_maps, core_ids=list(range(8)))
    return np.stack([res.results[b]["y"] for b in range(8)], axis=0)


# version 03 (final)
# speedup vs baseline: 1.5282x; 1.1345x over previous
import contextlib
import numpy as np
import ml_dtypes
import concourse.bass as bass
import concourse.mybir as mybir
from concourse.bass import ds
from concourse.bass_utils import run_bass_kernel_spmd

F32 = mybir.dt.float32
BF16 = mybir.dt.bfloat16
AF = mybir.ActivationFunctionType
ALU = mybir.AluOpType
AX = mybir.AxisListType

T, C, KC, TC = 2048, 1024, 8, 16
HP, DH, QW = 8, 64, 512
E, DFF, FG, G, FCG, NTT, TTW = 8, 4096, 512, 8, 4, 8, 256

ST_A, ST_B = 0, 512
AV_STD, AV_P = 1024, 2048
BCAST = 3072
HPS = 2048

SEED = {"sAV": 2, "sAVE": 2, "sSl": 2, "sHw": 2, "sW2": 1, "sEX": 1}


def build(nc, stage=99, taps=()):
    x_d = nc.dram_tensor("x", [T, C], F32, kind="ExternalInput").ap()
    wqT = nc.dram_tensor("wqT", [C, C], F32, kind="ExternalInput").ap()
    wkT = nc.dram_tensor("wkT", [C, C], F32, kind="ExternalInput").ap()
    wkpT = nc.dram_tensor("wkpT", [C, C], F32, kind="ExternalInput").ap()
    wvT = nc.dram_tensor("wvT", [C, C], F32, kind="ExternalInput").ap()
    woT = nc.dram_tensor("woT", [C, C], F32, kind="ExternalInput").ap()
    alpha_d = nc.dram_tensor("alpha", [1, 1], F32, kind="ExternalInput").ap()
    wg_d = nc.dram_tensor("wg", [C, E], F32, kind="ExternalInput").ap()
    w1_d = nc.dram_tensor("w1", [E, C, DFF], F32, kind="ExternalInput").ap()
    w2_d = nc.dram_tensor("w2", [E, DFF, C], F32, kind="ExternalInput").ap()
    w3_d = nc.dram_tensor("w3", [E, C, DFF], F32, kind="ExternalInput").ap()
    y_d = nc.dram_tensor("y", [T, C], F32, kind="ExternalOutput").ap()
    tap_d = {}
    for tname, tshape, tdt in taps:
        tap_d[tname] = nc.dram_tensor("tap_" + tname, list(tshape), tdt,
                                      kind="ExternalOutput").ap()

    id_f32 = nc.inline_tensor(np.eye(128, dtype=np.float32), "c_idf").ap()
    ones_fc = nc.inline_tensor(np.ones((128, 128), np.float32), "c_onesf").ap()
    ones_f1 = nc.inline_tensor(np.ones((128, 1), np.float32), "c_onesf1").ap()
    sel_np = np.zeros((8, 8 * 128), np.float32)
    for e in range(8):
        sel_np[e, e * 128:(e + 1) * 128] = 1.0
    sel_c = nc.inline_tensor(sel_np, "c_sel8").ap()
    iota_np = np.tile(np.arange(8, dtype=np.float32)[None, :], (128, 1))
    iota_c = nc.inline_tensor(iota_np, "c_iota8").ap()
    iotp_c = nc.inline_tensor(iota_np + 1000.0, "c_iotaP").ap()

    ctx = contextlib.ExitStack()
    with ctx:
        xf32 = ctx.enter_context(nc.sbuf_tensor("xf32", [128, KC * T], F32))
        xbf = ctx.enter_context(nc.sbuf_tensor("xbf", [128, KC * T], BF16))
        ytm = ctx.enter_context(nc.sbuf_tensor("ytm", [128, 2 * 1024], F32))
        ysq = ctx.enter_context(nc.sbuf_tensor("ysq", [128, 2 * 1024], F32))
        idf = ctx.enter_context(nc.sbuf_tensor("idf", [128, 128], F32))
        idb = ctx.enter_context(nc.sbuf_tensor("idb", [128, 128], BF16))
        onesf = ctx.enter_context(nc.sbuf_tensor("onesf", [128, 128], F32))
        onesb = ctx.enter_context(nc.sbuf_tensor("onesb", [128, 1], BF16))
        ones64 = ctx.enter_context(nc.sbuf_tensor("ones64", [128, 64], BF16))
        selb = ctx.enter_context(nc.sbuf_tensor("selb", [8, 8 * 128], F32))
        iota8 = ctx.enter_context(nc.sbuf_tensor("iota8", [128, 8], F32))
        iotaP = ctx.enter_context(nc.sbuf_tensor("iotaP", [128, 8], F32))
        alpha_a = ctx.enter_context(nc.sbuf_tensor("alpha_a", [128, 2], F32))
        stat = ctx.enter_context(nc.sbuf_tensor("stat", [128, 10], F32))
        psum = ctx.enter_context(nc.psum_tensor("arena", [128, 4096], F32))

        sem_names = [
            "ldx0", "ldx1", "cl", "clg", "sMS", "sXT", "sXE", "sALP", "sALPE",
            "watq", "watk", "watkp", "watv", "wol0", "wol1", "wgl", "sPJ", "sPE", "sPV",
            "sST", "sEX", "sAV", "sAVE", "sRC", "sBC", "sBL", "sVM",
            "sWO", "sWOE",
            "sL1T", "sL1C", "sLNa", "sLNb", "sL1N", "sL1B", "sL1E",
            "sRT", "sRTE", "sREX", "sRTT", "sRTC",
            "sWB", "sWBE", "sWC", "wf0", "wf1", "sA", "sB", "sSl", "sHw", "sW2", "sEv",
            "sFT", "sFC", "sFN", "st", "sty0", "sty1",
        ]
        sm = {n: ctx.enter_context(nc.semaphore(n)) for n in sem_names}
        blk = ctx.enter_context(nc.Block())

        n_stores = [0]

        def tap(sync, name, src_ap, wait_sem, wait_val):
            if name in tap_d:
                sync.wait_ge(sm[wait_sem], wait_val)
                sync.dma_start(out=tap_d[name], in_=src_ap).then_inc(sm["st"], 16)
                n_stores[0] += 1

        @blk.sync
        def _(sync):
            for dst, src in ((idf[:], id_f32), (onesf[:], ones_fc),
                             (selb[:], sel_c), (iota8[:], iota_c),
                             (iotaP[:], iotp_c), (alpha_a[0:1, 0:1], alpha_d)):
                sync.dma_start(out=dst, in_=src).then_inc(sm["cl"], 16)
            for tc in range(TC):
                if tc >= 2:
                    sync.wait_ge(sm["sXT"], tc - 1)
                sync.dma_start(out=ytm[:, (tc % 2) * 1024:(tc % 2) * 1024 + 1024],
                               in_=x_d[tc * 128:(tc + 1) * 128, :]).then_inc(sm["ldx%d" % (tc % 2)], 16)

        @blk.gpsimd
        def _(g):
            for s, v in SEED.items():
                g.sem_inc(sm[s], v)
            g.dma_start(out=idb[:], in_=id_f32).then_inc(sm["clg"], 16)
            g.dma_start(out=onesb[:], in_=ones_f1).then_inc(sm["clg"], 16)
            g.dma_start(out=ones64[:], in_=ones_fc[:, 0:64] if hasattr(ones_fc, '__getitem__') else ones_fc).then_inc(sm["clg"], 16)

        @blk.tensor
        def _(tensor):
            tensor.wait_ge(sm["cl"], 16 * 6)
            tensor.wait_ge(sm["clg"], 16 * 3)
            tensor.wait_ge(sm["sMS"], 1)
            tensor.matmul(psum[:, 2048:2049], onesf[0:1, 0:128], alpha_a[0:1, 0:1],
                          start=True, stop=True, skip_group_check=True).then_inc(sm["sALP"], 1)
            for tc in range(TC):
                tensor.wait_ge(sm["ldx%d" % (tc % 2)], 16 * (tc // 2 + 1))
                if tc >= 2:
                    tensor.wait_ge(sm["sXE"], tc - 1)
                pc = (tc % 2) * 1024
                for cc in range(KC):
                    ins = tensor.transpose(psum[:, pc + cc * 128:pc + cc * 128 + 128],
                                           ytm[:, (tc % 2) * 1024 + cc * 128:(tc % 2) * 1024 + cc * 128 + 128],
                                           idf[:])
                ins.then_inc(sm["sXT"], 1)

        @blk.vector
        def _(vector):
            vector.memset(psum[:, :], 0.0).then_inc(sm["sMS"], 1)
            vector.wait_ge(sm["sALP"], 1)
            vector.tensor_copy(alpha_a[:, 0:1], psum[:, 2048:2049])
            vector.drain()
            vector.tensor_scalar(alpha_a[:, 1:2], alpha_a[:, 0:1], -1.0, 1.0,
                                 ALU.mult, ALU.add).then_inc(sm["sALPE"], 1)
            vector.drain()
            for tc in range(TC):
                vector.wait_ge(sm["sXT"], tc + 1)
                pc = (tc % 2) * 1024
                pin = psum[:, pc:pc + 1024].rearrange("p (cc t) -> p cc t", cc=KC)
                vector.tensor_copy(
                    xf32[:, :].rearrange("p (cc t) -> p cc t", cc=KC)[:, :, tc * 128:tc * 128 + 128],
                    pin)
                vector.tensor_copy(
                    xbf[:, :].rearrange("p (cc t) -> p cc t", cc=KC)[:, :, tc * 128:tc * 128 + 128],
                    pin).then_inc(sm["sXE"], 1)

        actx = contextlib.ExitStack()
        with actx:
            wq_p = actx.enter_context(nc.sbuf_tensor("wq_p", [128, 2 * 1024], BF16))
            wk_p = actx.enter_context(nc.sbuf_tensor("wk_p", [128, 2 * 1024], BF16))
            wkp_p = actx.enter_context(nc.sbuf_tensor("wkp_p", [128, 2 * 1024], BF16))
            wv_p = actx.enter_context(nc.sbuf_tensor("wv_p", [128, 2 * 1024], BF16))
            q_fm = actx.enter_context(nc.sbuf_tensor("q_fm", [128, T], BF16))
            k_fm = actx.enter_context(nc.sbuf_tensor("k_fm", [128, T], BF16))
            kp_fm = actx.enter_context(nc.sbuf_tensor("kp_fm", [128, T], BF16))
            v_p = actx.enter_context(nc.sbuf_tensor("v_p", [128, 16 * 130], BF16))
            attnT = actx.enter_context(nc.sbuf_tensor("attnT", [128, 2 * 1024], BF16))
            av_sb = actx.enter_context(nc.sbuf_tensor("av_sb", [128, 2048], F32))
            rcp_sb = actx.enter_context(nc.sbuf_tensor("rcp_sb", [128, 1024], F32))
            rb = actx.enter_context(nc.sbuf_tensor("rb", [128, 1024], F32))
            wo_p = actx.enter_context(nc.sbuf_tensor("wo_p", [128, 2 * 1024], BF16))
            attn_out = actx.enter_context(nc.sbuf_tensor("attn_out", [128, HP * T], BF16))

            if stage >= 1:
                @blk.gpsimd
                def _(g):
                    for hp in range(HP):
                        par = hp % 2
                        kinds = (("watq", wq_p, wqT, 1), ("watk", wk_p, wkT, 6),
                                 ("watkp", wkp_p, wkpT, 11), ("watv", wv_p, wvT, 19))
                        for wsem, wten, wsrc, gate in kinds:
                            if hp >= 1:
                                g.wait_ge(sm["sPJ"], 19 * (hp - 1) + gate)
                            g.dma_start(
                                out=wten[:, par * 1024:(par + 1) * 1024].rearrange(
                                    "p (kc c) -> p kc c", kc=KC),
                                in_=wsrc.rearrange("(kc p) c -> p kc c", p=128)[:, :, hp * 128:(hp + 1) * 128],
                            ).then_inc(sm[wsem], 16)
                    if stage >= 2:
                        for mc in range(KC):
                            if mc >= 2:
                                g.wait_ge(sm["sWO"], mc - 1)
                            g.dma_start(
                                out=wo_p[:, (mc % 2) * 1024:(mc % 2) * 1024 + 1024].rearrange(
                                    "p (kc c) -> p kc c", kc=KC),
                                in_=woT.rearrange("(kc p) c -> p kc c", p=128)[:, :, mc * 128:(mc + 1) * 128],
                            ).then_inc(sm["wol%d" % (mc % 2)], 16)

                @blk.tensor
                def _(tensor):
                    tensor.wait_ge(sm["sXE"], TC)
                    tensor.wait_ge(sm["sALPE"], 1)
                    rWAT = tensor.alloc_register("rWAT")
                    rPVB = tensor.alloc_register("rPVB")
                    rT1 = tensor.alloc_register("rT1")
                    rPEW = tensor.alloc_register("rPEW")
                    rPEk = tensor.alloc_register("rPEk")
                    rKCB = tensor.alloc_register("rKCB")
                    rEX = tensor.alloc_register("rEX")
                    rAVE = tensor.alloc_register("rAVE")
                    rRC = tensor.alloc_register("rRC")
                    rT2 = tensor.alloc_register("rT2")
                    tensor.reg_mov(rWAT, 0)
                    tensor.reg_mov(rPVB, 0)
                    tensor.reg_mov(rPEW, 0)
                    tensor.reg_mov(rKCB, 0)
                    tensor.reg_mov(rAVE, 0)
                    tensor.reg_mov(rRC, 0)
                    with tensor.Fori(0, HP // 2) as hpi:
                        for hpar in range(2):
                            po = hpar * 1024
                            tensor.reg_alu(rT1, rKCB, 1, ALU.add)
                            tensor.wait_ge(sm["sEX"], rT1)
                            tensor.reg_alu(rT2, rAVE, 2, ALU.add)
                            tensor.wait_ge(sm["sAVE"], rT2)
                            tensor.reg_add(rWAT, rWAT, 16)
                            vtile = [0]

                            def vblock(nt, tensor=tensor, po=po, vtile=vtile):
                                for t in range(vtile[0], vtile[0] + nt):
                                    tensor.wait_ge(sm["watv"], rWAT)
                                    if t >= 2:
                                        tensor.reg_alu(rT1, rPVB, t - 1, ALU.add)
                                        tensor.wait_ge(sm["sPV"], rT1)
                                    vp = 2048 + (t % 2) * 512
                                    for kcj in range(KC):
                                        ins = tensor.matmul(
                                            psum[:, vp:vp + 128],
                                            xbf[:, kcj * T + t * 128:kcj * T + t * 128 + 128],
                                            wv_p[:, po + kcj * 128:po + kcj * 128 + 128],
                                            start=(kcj == 0), stop=(kcj == KC - 1),
                                            skip_group_check=True)
                                    ins.then_inc(sm["sPJ"], 1)
                                vtile[0] += nt

                            for wi, (wsn, wt) in enumerate((("watq", wq_p), ("watk", wk_p), ("watkp", wkp_p))):
                                tensor.wait_ge(sm[wsn], rWAT)
                                if wi >= 1:
                                    tensor.reg_add(rPEW, rPEW, 1)
                                    tensor.wait_ge(sm["sPE"], rPEW)
                                for kcj in range(KC):
                                    for nb in range(4):
                                        ins = tensor.matmul(
                                            psum[:, nb * 512:nb * 512 + 512],
                                            wt[:, po + kcj * 128:po + kcj * 128 + 128],
                                            xbf[:, kcj * T + nb * 512:kcj * T + nb * 512 + 512],
                                            start=(kcj == 0), stop=(kcj == KC - 1),
                                            skip_group_check=True)
                                ins.then_inc(sm["sPJ"], 1)
                                vblock(4 if wi < 2 else 8)

                            tensor.reg_add(rPEW, rPEW, 1)
                            tensor.wait_ge(sm["sPE"], rPEW)
                            tensor.reg_alu(rT1, rPVB, 16, ALU.add)
                            tensor.wait_ge(sm["sPV"], rT1)
                            for qh in range(4):
                                tensor.reg_add(rRC, rRC, 1)

                                def bcast_prev(tensor=tensor, hpi=hpi, hpar=hpar, qh=qh):
                                    if hpar == 0 and qh == 0:
                                        tensor.reg_alu(rT2, hpi, 1, ALU.mult)
                                        guard = tensor.If_ne(rT2, 0)
                                        guard.__enter__()
                                    tensor.reg_alu(rT2, rRC, 1, ALU.subtract)
                                    tensor.wait_ge(sm["sRC"], rT2)
                                    tensor.reg_alu(rT1, rT2, 1, ALU.subtract)
                                    tensor.wait_ge(sm["sBL"], rT1)
                                    for mi in range(2):
                                        bco = BCAST + mi * 512
                                        ins = tensor.matmul(
                                            psum[0:64, bco:bco + 512],
                                            onesf[64:65, 0:64], rcp_sb[64:65, mi * 512:mi * 512 + 512],
                                            start=True, stop=True, skip_group_check=True)
                                        ins = tensor.matmul(
                                            psum[64:128, bco:bco + 512],
                                            onesf[0:1, 0:64], rcp_sb[0:1, mi * 512:mi * 512 + 512],
                                            start=True, stop=True, skip_group_check=True)
                                    ins.then_inc(sm["sBC"], 1)
                                    if hpar == 0 and qh == 0:
                                        guard.__exit__(None, None, None)

                                tensor.reg_add(rAVE, rAVE, 2)
                                tensor.wait_ge(sm["sAVE"], rAVE)
                                tensor.reg_alu(rEX, rKCB, qh * 32, ALU.add)

                                def st_of(j, tensor=tensor, qh=qh):
                                    mat, kc = j // 16, j % 16
                                    kf = kp_fm if mat else k_fm
                                    stb = 0 if kc % 2 == 0 else (2048 if mat == 0 else 1024)
                                    ins = tensor.matmul(
                                        psum[:, stb:stb + 512],
                                        kf[0:64, kc * 128:kc * 128 + 128],
                                        q_fm[0:64, qh * QW:qh * QW + QW],
                                        start=True, stop=True, tile_position=(0, 0),
                                        skip_group_check=True)
                                    ins = tensor.matmul(
                                        psum[:, stb + 512:stb + 1024],
                                        kf[64:128, kc * 128:kc * 128 + 128],
                                        q_fm[64:128, qh * QW:qh * QW + QW],
                                        start=True, stop=True, tile_position=(64, 0),
                                        skip_group_check=True)
                                    ins.then_inc(sm["sST"], 1)

                                def av_of(j, tensor=tensor):
                                    mat, kc = j // 16, j % 16
                                    avo = AV_P if mat else AV_STD
                                    slot = (kc % 2) * 1024
                                    ins = tensor.matmul(
                                        psum[0:65, avo:avo + 512],
                                        v_p[:, kc * 130:kc * 130 + 65],
                                        attnT[:, slot:slot + 512],
                                        start=(kc == 0), stop=(kc == 15),
                                        skip_group_check=True)
                                    ins = tensor.matmul(
                                        psum[0:64, avo + 512:avo + 1024],
                                        ones64[:],
                                        attnT[:, slot + 512:slot + 1024],
                                        start=(kc == 0), stop=(kc == 15),
                                        skip_group_check=True)
                                    ins = tensor.matmul(
                                        psum[64:128, avo + 512:avo + 1024],
                                        v_p[:, kc * 130 + 65:kc * 130 + 129],
                                        attnT[:, slot + 512:slot + 1024],
                                        start=(kc == 0), stop=(kc == 15),
                                        skip_group_check=True)
                                    ins.then_inc(sm["sAV"], 1)

                                for j in range(32):
                                    if j == 8:
                                        bcast_prev()
                                    if j == 17:
                                        tensor.reg_alu(rT1, rAVE, 1, ALU.add)
                                        tensor.wait_ge(sm["sAVE"], rT1)
                                    tensor.wait_ge(sm["sEX"], rEX)
                                    st_of(j)
                                    tensor.reg_add(rEX, rEX, 1)
                                    if j >= 1:
                                        tensor.wait_ge(sm["sEX"], rEX)
                                        av_of(j - 1)
                                tensor.reg_add(rEX, rEX, 1)
                                tensor.wait_ge(sm["sEX"], rEX)
                                av_of(31)
                            tensor.reg_add(rPVB, rPVB, 16)
                            tensor.reg_add(rKCB, rKCB, 128)
                    tensor.wait_ge(sm["sRC"], 32)
                    tensor.wait_ge(sm["sBL"], 31)
                    for mi in range(2):
                        bco = BCAST + mi * 512
                        ins = tensor.matmul(psum[0:64, bco:bco + 512],
                                            onesf[64:65, 0:64], rcp_sb[64:65, mi * 512:mi * 512 + 512],
                                            start=True, stop=True, skip_group_check=True)
                        ins = tensor.matmul(psum[64:128, bco:bco + 512],
                                            onesf[0:1, 0:64], rcp_sb[0:1, mi * 512:mi * 512 + 512],
                                            start=True, stop=True, skip_group_check=True)
                    ins.then_inc(sm["sBC"], 1)

                @blk.scalar
                def _(scalar):
                    rW = scalar.alloc_register("rW")
                    rKC = scalar.alloc_register("rKC")
                    scalar.reg_mov(rKC, 1)
                    with scalar.Fori(0, HP // 2) as hpi:
                        for hpar in range(2):
                            with scalar.Fori(0, 4) as qh:
                                scalar.reg_alu(rW, qh, 32, ALU.mult)
                                scalar.reg_alu(rW, rW, rKC, ALU.add)
                                for mat in range(2):
                                    for kc in range(16):
                                        stb = 0 if kc % 2 == 0 else (2048 if mat == 0 else 1024)
                                        scalar.wait_ge(sm["sST"], rW)
                                        scalar.wait_ge(sm["sAV"], rW)
                                        scalar.activation(
                                            attnT[:, (kc % 2) * 1024:(kc % 2) * 1024 + 1024],
                                            psum[:, stb:stb + 1024],
                                            AF.Exp).then_inc(sm["sEX"], 1)
                                        scalar.reg_add(rW, rW, 1)
                            scalar.reg_add(rKC, rKC, 128)

                @blk.vector
                def _(vector):
                    vector.memset(
                        v_p[:, :].rearrange("p (t x c) -> p t x c", t=16, x=2)[:, :, :, 64:65], 1.0)

                    def blend_v2(bq, vector=vector):
                        co = bq * QW
                        vector.tensor_copy(rb[:], psum[:, BCAST:BCAST + 1024])
                        vector.drain()
                        vector.tensor_mul(av_sb[0:64, 0:512], av_sb[0:64, 0:512], rb[0:64, 0:512])
                        vector.tensor_mul(av_sb[64:128, 512:1024], av_sb[64:128, 512:1024], rb[64:128, 0:512])
                        vector.tensor_mul(av_sb[0:64, 1024:1536], av_sb[0:64, 1024:1536], rb[0:64, 512:1024])
                        vector.tensor_mul(av_sb[64:128, 1536:2048], av_sb[64:128, 1536:2048], rb[64:128, 512:1024])
                        vector.drain()
                        vector.tensor_add(attn_out[0:64, co:co + QW],
                                          av_sb[0:64, 0:512], av_sb[0:64, 1024:1536])
                        vector.tensor_add(attn_out[64:128, co:co + QW],
                                          av_sb[64:128, 512:1024], av_sb[64:128, 1536:2048]
                                          ).then_inc(sm["sBL"], 1)
                        vector.drain()

                    for hp in range(HP):
                        PJ = hp * 19
                        if hp >= 1:
                            vector.wait_ge(sm["sST"], 128 * hp)
                            vector.wait_ge(sm["sAV"], 128 * hp + SEED["sAV"])

                        def vevac(t, hp=hp, vector=vector):
                            gidx = hp * 19 + (2 + t if t < 4 else (3 + t if t < 8 else 4 + t))
                            vector.wait_ge(sm["sPJ"], gidx)
                            vp = 2048 + (t % 2) * 512
                            vector.tensor_copy(
                                v_p[:, t * 130:t * 130 + 130].rearrange(
                                    "p (h d) -> p h d", h=2)[:, :, 0:64],
                                psum[:, vp:vp + 128].rearrange("p (h d) -> p h d", h=2),
                            ).then_inc(sm["sPV"], 1)

                        vector.wait_ge(sm["sPJ"], PJ + 1)
                        vector.tensor_scalar_mul(q_fm[:], psum[:, 0:2048], 0.125).then_inc(sm["sPE"], 1)
                        for t in range(4):
                            vevac(t)
                        vector.wait_ge(sm["sPJ"], PJ + 6)
                        vector.tensor_copy(k_fm[:], psum[:, 0:2048]).then_inc(sm["sPE"], 1)
                        for t in range(4, 8):
                            vevac(t)
                        vector.wait_ge(sm["sPJ"], PJ + 11)
                        vector.tensor_copy(kp_fm[:], psum[:, 0:2048]).then_inc(sm["sPE"], 1)
                        for t in range(8, 16):
                            vevac(t)
                        for qh in range(4):
                            bq = hp * 4 + qh
                            if bq >= 1:
                                vector.wait_ge(sm["sBC"], bq)
                                blend_v2(bq - 1)
                            vector.wait_ge(sm["sAV"], 16 * (2 * bq + 1) + SEED["sAV"])
                            vector.tensor_copy(av_sb[:, 0:1024],
                                               psum[:, AV_STD:AV_STD + 1024]).then_inc(sm["sAVE"], 1)
                            vector.wait_ge(sm["sAV"], 16 * (2 * bq + 2) + SEED["sAV"])
                            vector.tensor_copy(av_sb[:, 1024:2048],
                                               psum[:, AV_P:AV_P + 1024]).then_inc(sm["sAVE"], 1)
                            vector.drain()
                            vector.reciprocal(rcp_sb[64:65, 0:512], av_sb[64:65, 0:512])
                            vector.reciprocal(rcp_sb[64:65, 512:1024], av_sb[64:65, 1024:1536])
                            vector.reciprocal(rcp_sb[0:1, 0:512], av_sb[0:1, 512:1024])
                            vector.reciprocal(rcp_sb[0:1, 512:1024], av_sb[0:1, 1536:2048])
                            vector.drain()
                            vector.tensor_scalar_mul(rcp_sb[64:65, 0:512], rcp_sb[64:65, 0:512],
                                                     alpha_a[64:65, 0:1])
                            vector.tensor_scalar_mul(rcp_sb[0:1, 0:512], rcp_sb[0:1, 0:512],
                                                     alpha_a[0:1, 0:1])
                            vector.tensor_scalar_mul(rcp_sb[64:65, 512:1024], rcp_sb[64:65, 512:1024],
                                                     alpha_a[64:65, 1:2])
                            vector.tensor_scalar_mul(rcp_sb[0:1, 512:1024], rcp_sb[0:1, 512:1024],
                                                     alpha_a[0:1, 1:2]).then_inc(sm["sRC"], 1)
                    vector.wait_ge(sm["sBC"], 32)
                    blend_v2(31)

            if stage >= 2:
                @blk.tensor
                def _(tensor):
                    tensor.wait_ge(sm["sBL"], 32)
                    for mc in range(KC):
                        tensor.wait_ge(sm["wol%d" % (mc % 2)], 16 * (mc // 2 + 1))
                        if mc >= 2:
                            tensor.wait_ge(sm["sWOE"], mc - 1)
                        pc = (mc % 2) * 2048
                        for kcj in range(KC):
                            for nb in range(4):
                                ins = tensor.matmul(
                                    psum[:, pc + nb * 512:pc + nb * 512 + 512],
                                    wo_p[:, (mc % 2) * 1024 + kcj * 128:(mc % 2) * 1024 + kcj * 128 + 128],
                                    attn_out[:, kcj * T + nb * 512:kcj * T + nb * 512 + 512],
                                    start=(kcj == 0), stop=(kcj == KC - 1),
                                    skip_group_check=True)
                        ins.then_inc(sm["sWO"], 1)

                @blk.vector
                def _(vector):
                    for mc in range(KC):
                        vector.wait_ge(sm["sWO"], mc + 1)
                        pc = (mc % 2) * 2048
                        vector.tensor_add(xf32[:, mc * T:mc * T + T],
                                          xf32[:, mc * T:mc * T + T],
                                          psum[:, pc:pc + 2048]).then_inc(sm["sWOE"], 1)

            if stage >= 3:
                @blk.tensor
                def _(tensor):
                    tensor.wait_ge(sm["sWOE"], KC)
                    for tc in range(TC):
                        if tc >= 2:
                            tensor.wait_ge(sm["sL1C"], tc - 1)
                        pc = (tc % 2) * 1024
                        for cc in range(KC):
                            ins = tensor.transpose(
                                psum[:, pc + cc * 128:pc + cc * 128 + 128],
                                xf32[:, cc * T + tc * 128:cc * T + tc * 128 + 128], idf[:])
                        ins.then_inc(sm["sL1T"], 1)
                        tensor.wait_ge(sm["sL1N"], tc + 1)
                        if tc >= 2:
                            tensor.wait_ge(sm["sL1E"], tc - 1)
                        pb = 2048 + (tc % 2) * 1024
                        for cc in range(KC):
                            ins = tensor.transpose(
                                psum[:, pb + cc * 128:pb + cc * 128 + 128],
                                ysq[:, (tc % 2) * 1024 + cc * 128:(tc % 2) * 1024 + cc * 128 + 128], idf[:])
                        ins.then_inc(sm["sL1B"], 1)

                @blk.scalar
                def _(scalar):
                    for tc in range(TC):
                        scalar.wait_ge(sm["sLNa"], tc + 1)
                        scalar.activation(stat[:, 8:9], stat[:, 7:8], AF.Sqrt).then_inc(sm["sLNb"], 1)

                @blk.vector
                def _(vector):
                    for tc in range(TC):
                        sl = (tc % 2) * 1024
                        vector.wait_ge(sm["sL1T"], tc + 1)
                        if tc >= 2:
                            vector.wait_ge(sm["sL1B"], tc - 1)
                        vector.tensor_copy(ytm[:, sl:sl + 1024], psum[:, sl:sl + 1024]).then_inc(sm["sL1C"], 1)
                        vector.drain()
                        vector.reduce_sum(stat[:, 0:1], ytm[:, sl:sl + 1024], axis=AX.X)
                        vector.tensor_mul(ysq[:, sl:sl + 1024], ytm[:, sl:sl + 1024], ytm[:, sl:sl + 1024])
                        vector.drain()
                        vector.reduce_sum(stat[:, 1:2], ysq[:, sl:sl + 1024], axis=AX.X)
                        vector.drain()
                        vector.tensor_scalar_mul(stat[:, 2:3], stat[:, 0:1], 1.0 / C)
                        vector.tensor_scalar_mul(stat[:, 3:4], stat[:, 1:2], 1.0 / C)
                        vector.drain()
                        vector.tensor_mul(stat[:, 4:5], stat[:, 2:3], stat[:, 2:3])
                        vector.drain()
                        vector.tensor_sub(stat[:, 5:6], stat[:, 3:4], stat[:, 4:5])
                        vector.drain()
                        vector.tensor_scalar_add(stat[:, 6:7], stat[:, 5:6], 1e-5)
                        vector.drain()
                        vector.reciprocal(stat[:, 7:8], stat[:, 6:7]).then_inc(sm["sLNa"], 1)
                        vector.wait_ge(sm["sLNb"], tc + 1)
                        vector.drain()
                        vector.tensor_scalar(ysq[:, sl:sl + 1024], ytm[:, sl:sl + 1024],
                                             stat[:, 2:3], stat[:, 8:9],
                                             ALU.subtract, ALU.mult).then_inc(sm["sL1N"], 1)
                        vector.wait_ge(sm["sL1B"], tc + 1)
                        pb = 2048 + (tc % 2) * 1024
                        pin = psum[:, pb:pb + 1024].rearrange("p (cc t) -> p cc t", cc=KC)
                        vector.tensor_copy(
                            xf32[:, :].rearrange("p (cc t) -> p cc t", cc=KC)[:, :, tc * 128:tc * 128 + 128],
                            pin)
                        vector.tensor_copy(
                            xbf[:, :].rearrange("p (cc t) -> p cc t", cc=KC)[:, :, tc * 128:tc * 128 + 128],
                            pin).then_inc(sm["sL1E"], 1)
                        vector.drain()

            @blk.sync
            def _(sync):
                tap(sync, "xf32", xf32[:], "sXE", TC)
                tap(sync, "qfm", q_fm[:], "sPE", 22)
                tap(sync, "kfm", k_fm[:], "sPE", 23)
                tap(sync, "vp", v_p[:], "sPV", 128)
                tap(sync, "attn_out", attn_out[:], "sBL", 32)
                tap(sync, "x2", xf32[:], "sL1E", TC)
                tap(sync, "x2bf", xbf[:], "sL1E", TC)

        fctx = contextlib.ExitStack()
        with fctx:
            w1g = fctx.enter_context(nc.sbuf_tensor("w1g", [128, 2 * KC * FG], BF16))
            w3g = fctx.enter_context(nc.sbuf_tensor("w3g", [128, 2 * KC * FG], BF16))
            w2g = fctx.enter_context(nc.sbuf_tensor("w2g", [128, 2 * FCG * C], BF16))
            wgb = fctx.enter_context(nc.sbuf_tensor("wgb", [128, KC * 8], BF16))
            wbc_all = fctx.enter_context(nc.sbuf_tensor("wbc_all", [128, E * T], BF16))
            wbc = fctx.enter_context(nc.sbuf_tensor("wbc", [128, T], BF16))
            h_w = fctx.enter_context(nc.sbuf_tensor("h_w", [128, 2 * FCG * TTW], BF16))
            s1 = fctx.enter_context(nc.sbuf_tensor("s1", [128, 2 * TTW], BF16))
            lg_sb = ysq
            combT = ysq
            et_sb = ytm[:, 0:128]
            comb_tm = ytm[:, 128:256]
            rwk = ytm[:, 256:304]

            if stage >= 4:
                @blk.gpsimd
                def _(g):
                    g.wait_ge(sm["sBL"], 32)
                    g.dma_start(out=wgb[:].rearrange("p (kc e) -> p kc e", kc=KC),
                                in_=wg_d.rearrange("(kc p) e -> p kc e", p=128)).then_inc(sm["wgl"], 16)

                @blk.tensor
                def _(tensor):
                    tensor.wait_ge(sm["sL1E"], TC)
                    tensor.wait_ge(sm["wgl"], 16)
                    for nb in range(4):
                        for kcj in range(KC):
                            ins = tensor.matmul(
                                psum[0:8, nb * 512:nb * 512 + 512],
                                wgb[:, kcj * 8:kcj * 8 + 8],
                                xbf[:, kcj * T + nb * 512:kcj * T + nb * 512 + 512],
                                start=(kcj == 0), stop=(kcj == KC - 1),
                                skip_group_check=True)
                    ins.then_inc(sm["sRT"], 1)
                    tensor.wait_ge(sm["sRTE"], 1)
                    for c in range(16):
                        ins = tensor.transpose(psum[:, 2048 + c * 8:2048 + c * 8 + 8],
                                               lg_sb[0:8, c * 128:c * 128 + 128], idf[0:8, 0:8])
                    ins.then_inc(sm["sRT"], 1)
                    tensor.wait_ge(sm["sRTT"], 1)
                    for c in range(16):
                        ins = tensor.transpose(psum[0:8, c * 128:c * 128 + 128],
                                               comb_tm[:, c * 8:c * 8 + 8], idf[:])
                    ins.then_inc(sm["sRT"], 1)

                @blk.scalar
                def _(scalar):
                    scalar.wait_ge(sm["sRT"], 2)
                    scalar.activation(et_sb[:], psum[:, 2048:2176], AF.Exp).then_inc(sm["sREX"], 1)

                @blk.vector
                def _(vector):
                    vector.wait_ge(sm["sRT"], 1)
                    vector.tensor_copy(lg_sb[0:8, 0:2048], psum[0:8, 0:2048]).then_inc(sm["sRTE"], 1)
                    vector.wait_ge(sm["sREX"], 1)
                    for c in range(16):
                        ec = et_sb[:, c * 8:c * 8 + 8]
                        vector.reduce_max(rwk[:, 0:1], ec, axis=AX.X)
                        vector.reduce_sum(rwk[:, 4:5], ec, axis=AX.X)
                        vector.drain()
                        vector.tensor_scalar(rwk[:, 8:16], ec, rwk[:, 0:1], None, ALU.is_equal)
                        vector.drain()
                        vector.scalar_tensor_tensor(rwk[:, 16:24], rwk[:, 8:16], -1000.0, iotaP[:],
                                                    ALU.mult, ALU.add)
                        vector.drain()
                        vector.tensor_reduce(rwk[:, 1:2], rwk[:, 16:24], axis=AX.X, op=ALU.min)
                        vector.drain()
                        vector.tensor_scalar(rwk[:, 8:16], iota8[:], rwk[:, 1:2], None, ALU.is_equal)
                        vector.drain()
                        vector.scalar_tensor_tensor(rwk[:, 16:24], rwk[:, 8:16], -1e9, ec,
                                                    ALU.mult, ALU.add)
                        vector.drain()
                        vector.reduce_max(rwk[:, 2:3], rwk[:, 16:24], axis=AX.X)
                        vector.drain()
                        vector.tensor_scalar(rwk[:, 24:32], rwk[:, 16:24], rwk[:, 2:3], None, ALU.is_equal)
                        vector.drain()
                        vector.scalar_tensor_tensor(rwk[:, 32:40], rwk[:, 24:32], -1000.0, iotaP[:],
                                                    ALU.mult, ALU.add)
                        vector.drain()
                        vector.tensor_reduce(rwk[:, 3:4], rwk[:, 32:40], axis=AX.X, op=ALU.min)
                        vector.drain()
                        vector.tensor_scalar(rwk[:, 24:32], iota8[:], rwk[:, 3:4], None, ALU.is_equal)
                        vector.tensor_add(rwk[:, 5:6], rwk[:, 0:1], rwk[:, 2:3])
                        vector.drain()
                        vector.scalar_tensor_tensor(rwk[:, 6:7], rwk[:, 4:5], 1e-8, rwk[:, 5:6],
                                                    ALU.mult, ALU.add)
                        vector.drain()
                        vector.reciprocal(rwk[:, 7:8], rwk[:, 6:7])
                        vector.drain()
                        vector.tensor_mul(rwk[:, 40:41], rwk[:, 0:1], rwk[:, 7:8])
                        vector.tensor_mul(rwk[:, 41:42], rwk[:, 2:3], rwk[:, 7:8])
                        vector.drain()
                        vector.tensor_scalar(rwk[:, 32:40], rwk[:, 8:16], rwk[:, 40:41], None, ALU.mult)
                        vector.drain()
                        vector.scalar_tensor_tensor(comb_tm[:, c * 8:c * 8 + 8],
                                                    rwk[:, 24:32], rwk[:, 41:42], rwk[:, 32:40],
                                                    ALU.mult, ALU.add)
                        vector.drain()
                    vector.nop().then_inc(sm["sRTT"], 1)
                    vector.wait_ge(sm["sRT"], 3)
                    vector.tensor_copy(combT[0:8, 0:2048], psum[0:8, 0:2048]).then_inc(sm["sRTC"], 1)

            if stage >= 5:
                @blk.gpsimd
                def _(g):
                    for n in range(E * G):
                        e, gg = n // G, n % G
                        par = n % 2
                        if gg == 0:
                            g.wait_ge(sm["sWBE"], e + 1)
                            if e >= 1:
                                g.wait_ge(sm["sHw"], 256 * e + SEED["sHw"])
                            g.dma_start(out=wbc[:], in_=wbc_all[:, e * T:(e + 1) * T]
                                        ).then_inc(sm["sWC"], 16)
                        if n < 2:
                            g.wait_ge(sm["sBL"], 32)
                        elif n % 2 == 0:
                            g.wait_ge(sm["sEv"], max(0, 16 * (n // 2 - 1) + 8))
                        else:
                            g.wait_ge(sm["sEv"], 16 * (n // 2))
                        g.dma_start(
                            out=w1g[:, par * KC * FG:(par + 1) * KC * FG].rearrange(
                                "p (kc f) -> p kc f", kc=KC),
                            in_=w1_d[e].rearrange("(kc p) f -> p kc f", p=128)[:, :, gg * FG:(gg + 1) * FG],
                        ).then_inc(sm["wf%d" % par], 16)
                        g.dma_start(
                            out=w3g[:, par * KC * FG:(par + 1) * KC * FG].rearrange(
                                "p (kc f) -> p kc f", kc=KC),
                            in_=w3_d[e].rearrange("(kc p) f -> p kc f", p=128)[:, :, gg * FG:(gg + 1) * FG],
                        ).then_inc(sm["wf%d" % par], 16)
                        g.dma_start(
                            out=w2g[:, par * FCG * C:(par + 1) * FCG * C].rearrange(
                                "p (fc c) -> p fc c", fc=FCG),
                            in_=w2_d[e, gg * FG:(gg + 1) * FG, :].rearrange("(fc p) c -> p fc c", p=128),
                        ).then_inc(sm["wf%d" % par], 16)

                @blk.tensor
                def _(tensor):
                    tensor.wait_ge(sm["sRTC"], 1)
                    for e in range(E):
                        if e >= 1:
                            tensor.wait_ge(sm["sWBE"], e)
                        for nb in range(4):
                            ins = tensor.matmul(psum[:, nb * 512:nb * 512 + 512],
                                                selb[:, e * 128:e * 128 + 128],
                                                combT[0:8, nb * 512:nb * 512 + 512],
                                                start=True, stop=True, skip_group_check=True)
                        ins.then_inc(sm["sWB"], 1)
                    tensor.wait_ge(sm["sWBE"], E)
                    rWF = tensor.alloc_register("rWF")
                    rSL = tensor.alloc_register("rSL")
                    rHW = tensor.alloc_register("rHW")
                    rEV = tensor.alloc_register("rEV")
                    rTF = tensor.alloc_register("rTF")
                    tensor.reg_mov(rWF, 0)
                    tensor.reg_mov(rSL, 0)
                    tensor.reg_mov(rHW, 0)
                    tensor.reg_mov(rEV, 0)
                    with tensor.Fori(0, E * G // 2) as p:
                        tensor.reg_add(rWF, rWF, 48)
                        for par in range(2):
                            tensor.wait_ge(sm["wf%d" % par], rWF)
                            for tt in range(NTT):
                                tto = tt * TTW
                                wo1 = par * KC * FG
                                wo2 = par * FCG * C
                                hwo = (tt % 2) * FCG * TTW
                                for fc in range(FCG):
                                    hA = HPS + (fc % 2) * 1024
                                    hB = hA + 512
                                    tensor.reg_add(rSL, rSL, 1)
                                    tensor.wait_ge(sm["sSl"], rSL)
                                    for kcj in range(KC):
                                        ins = tensor.matmul(
                                            psum[:, hA:hA + TTW],
                                            w1g[:, wo1 + kcj * FG + fc * 128:wo1 + kcj * FG + fc * 128 + 128],
                                            xbf[:, kcj * T + tto:kcj * T + tto + TTW],
                                            start=(kcj == 0), stop=(kcj == KC - 1),
                                            skip_group_check=True)
                                    ins.then_inc(sm["sA"], 1)
                                    tensor.reg_add(rHW, rHW, 1)
                                    tensor.wait_ge(sm["sHw"], rHW)
                                    for kcj in range(KC):
                                        ins = tensor.matmul(
                                            psum[:, hB:hB + TTW],
                                            w3g[:, wo1 + kcj * FG + fc * 128:wo1 + kcj * FG + fc * 128 + 128],
                                            xbf[:, kcj * T + tto:kcj * T + tto + TTW],
                                            start=(kcj == 0), stop=(kcj == KC - 1),
                                            skip_group_check=True)
                                    ins.then_inc(sm["sB"], 1)
                                if tt >= 1:
                                    wo2p = par * FCG * C
                                    hwop = ((tt - 1) % 2) * FCG * TTW
                                elif par == 1:
                                    wo2p = 0
                                    hwop = 1 * FCG * TTW
                                else:
                                    wo2p = 1 * FCG * C
                                    hwop = 1 * FCG * TTW
                                guardw = None
                                if par == 0 and tt == 0:
                                    tensor.reg_alu(rTF, p, 1, ALU.mult)
                                    guardw = tensor.If_ne(rTF, 0)
                                    guardw.__enter__()
                                tensor.reg_alu(rTF, rHW, 2, ALU.subtract)
                                tensor.wait_ge(sm["sHw"], rTF)
                                tensor.wait_ge(sm["sEv"], rEV)
                                for fc in range(FCG):
                                    for m in range(KC):
                                        ins = tensor.matmul(
                                            psum[:, m * TTW:m * TTW + TTW],
                                            w2g[:, wo2p + fc * C + m * 128:wo2p + fc * C + m * 128 + 128],
                                            h_w[:, hwop + fc * TTW:hwop + fc * TTW + TTW],
                                            start=(fc == 0 and m % 2 == 0), stop=(fc == FCG - 1),
                                            skip_group_check=True)
                                ins.then_inc(sm["sW2"], 1)
                                tensor.reg_add(rEV, rEV, 1)
                                if guardw is not None:
                                    guardw.__exit__(None, None, None)
                    tensor.wait_ge(sm["sHw"], 4 * 512 + SEED["sHw"])
                    tensor.wait_ge(sm["sEv"], 511)
                    for fc in range(FCG):
                        for m in range(KC):
                            ins = tensor.matmul(
                                psum[:, m * TTW:m * TTW + TTW],
                                w2g[:, FCG * C + fc * C + m * 128:FCG * C + fc * C + m * 128 + 128],
                                h_w[:, FCG * TTW + fc * TTW:FCG * TTW + fc * TTW + TTW],
                                start=(fc == 0 and m % 2 == 0), stop=(fc == FCG - 1),
                                skip_group_check=True)
                    ins.then_inc(sm["sW2"], 1)

                @blk.scalar
                def _(scalar):
                    rSA = scalar.alloc_register("rSA")
                    rHW = scalar.alloc_register("rHW2")
                    scalar.reg_mov(rSA, 0)
                    scalar.reg_mov(rHW, 0)
                    with scalar.Fori(0, E * G // 2) as p:
                        for par in range(2):
                            for tt in range(NTT):
                                for fc in range(FCG):
                                    scalar.reg_add(rSA, rSA, 1)
                                    scalar.wait_ge(sm["sA"], rSA)
                                    scalar.reg_add(rHW, rHW, 1)
                                    scalar.wait_ge(sm["sHw"], rHW)
                                    hA = HPS + (fc % 2) * 1024
                                    scalar.activation(s1[:, (fc % 2) * TTW:(fc % 2) * TTW + TTW],
                                                      psum[:, hA:hA + TTW],
                                                      AF.Silu).then_inc(sm["sSl"], 1)

                @blk.vector
                def _(vector):
                    for e in range(E):
                        vector.wait_ge(sm["sWB"], e + 1)
                        vector.tensor_copy(wbc_all[:, e * T:(e + 1) * T],
                                           psum[:, 0:2048]).then_inc(sm["sWBE"], 1)
                    rB = vector.alloc_register("fB")
                    rSl = vector.alloc_register("fSl")
                    rW2a = vector.alloc_register("fW2a")
                    rW2b = vector.alloc_register("fW2b")
                    rWC = vector.alloc_register("fWC")
                    rVT = vector.alloc_register("fVT")
                    vector.reg_mov(rB, 0)
                    vector.reg_mov(rSl, 2)
                    vector.reg_mov(rW2a, 0)
                    vector.reg_mov(rW2b, 2)
                    vector.reg_mov(rWC, 0)
                    with vector.Fori(0, E * G // 2) as p:
                        vector.reg_alu(rVT, p, 4, ALU.mod)
                        with vector.If_eq(rVT, 0):
                            vector.reg_add(rWC, rWC, 16)
                        vector.wait_ge(sm["sWC"], rWC)
                        for par in range(2):
                            for tt in range(NTT):
                                tto = tt * TTW
                                hwo = (tt % 2) * FCG * TTW
                                vector.wait_ge(sm["sW2"], rW2a)
                                vector.reg_add(rW2a, rW2a, 1)
                                for fc in range(FCG):
                                    hB = HPS + (fc % 2) * 1024 + 512
                                    vector.reg_add(rB, rB, 1)
                                    vector.wait_ge(sm["sB"], rB)
                                    vector.tensor_mul(psum[:, hB:hB + TTW], psum[:, hB:hB + TTW],
                                                      wbc[:, tto:tto + TTW])
                                    vector.drain()
                                    vector.reg_add(rSl, rSl, 1)
                                    vector.wait_ge(sm["sSl"], rSl)
                                    vector.tensor_mul(
                                        h_w[:, hwo + fc * TTW:hwo + fc * TTW + TTW],
                                        s1[:, (fc % 2) * TTW:(fc % 2) * TTW + TTW],
                                        psum[:, hB:hB + TTW]).then_inc(sm["sHw"], 1)
                                if tt >= 1:
                                    ttop = (tt - 1) * TTW
                                else:
                                    ttop = 7 * TTW
                                guardv = None
                                if par == 0 and tt == 0:
                                    vector.reg_alu(rVT, p, 1, ALU.mult)
                                    guardv = vector.If_ne(rVT, 0)
                                    guardv.__enter__()
                                vector.wait_ge(sm["sW2"], rW2b)
                                vector.reg_add(rW2b, rW2b, 1)
                                vector.tensor_add(
                                    xf32[:, :].rearrange("p (cc t) -> p cc t", cc=KC)[:, :, ttop:ttop + TTW],
                                    xf32[:, :].rearrange("p (cc t) -> p cc t", cc=KC)[:, :, ttop:ttop + TTW],
                                    psum[:, 0:2048].rearrange("p (m t) -> p m t", m=KC),
                                ).then_inc(sm["sEv"], 1)
                                if guardv is not None:
                                    guardv.__exit__(None, None, None)
                    vector.wait_ge(sm["sW2"], 512 + SEED["sW2"])
                    vector.tensor_add(
                        xf32[:, :].rearrange("p (cc t) -> p cc t", cc=KC)[:, :, 7 * TTW:7 * TTW + TTW],
                        xf32[:, :].rearrange("p (cc t) -> p cc t", cc=KC)[:, :, 7 * TTW:7 * TTW + TTW],
                        psum[:, 0:2048].rearrange("p (m t) -> p m t", m=KC),
                    ).then_inc(sm["sEv"], 1)

            if stage >= 6:
                @blk.tensor
                def _(tensor):
                    tensor.wait_ge(sm["sEv"], 512)
                    for tc in range(TC):
                        if tc >= 2:
                            tensor.wait_ge(sm["sFC"], tc - 1)
                        pc = (tc % 2) * 1024
                        for cc in range(KC):
                            ins = tensor.transpose(
                                psum[:, pc + cc * 128:pc + cc * 128 + 128],
                                xf32[:, cc * T + tc * 128:cc * T + tc * 128 + 128], idf[:])
                        ins.then_inc(sm["sFT"], 1)

                @blk.scalar
                def _(scalar):
                    for tc in range(TC):
                        scalar.wait_ge(sm["sLNa"], TC + tc + 1)
                        scalar.activation(stat[:, 8:9], stat[:, 7:8], AF.Sqrt).then_inc(sm["sLNb"], 1)

                @blk.vector
                def _(vector):
                    for tc in range(TC):
                        sl = (tc % 2) * 1024
                        vector.wait_ge(sm["sFT"], tc + 1)
                        vector.tensor_copy(ytm[:, sl:sl + 1024], psum[:, sl:sl + 1024]).then_inc(sm["sFC"], 1)
                        vector.drain()
                        vector.reduce_sum(stat[:, 0:1], ytm[:, sl:sl + 1024], axis=AX.X)
                        if tc >= 2:
                            vector.wait_ge(sm["sty%d" % (tc % 2)], 16 * (tc // 2))
                        vector.tensor_mul(ysq[:, sl:sl + 1024], ytm[:, sl:sl + 1024], ytm[:, sl:sl + 1024])
                        vector.drain()
                        vector.reduce_sum(stat[:, 1:2], ysq[:, sl:sl + 1024], axis=AX.X)
                        vector.drain()
                        vector.tensor_scalar_mul(stat[:, 2:3], stat[:, 0:1], 1.0 / C)
                        vector.tensor_scalar_mul(stat[:, 3:4], stat[:, 1:2], 1.0 / C)
                        vector.drain()
                        vector.tensor_mul(stat[:, 4:5], stat[:, 2:3], stat[:, 2:3])
                        vector.drain()
                        vector.tensor_sub(stat[:, 5:6], stat[:, 3:4], stat[:, 4:5])
                        vector.drain()
                        vector.tensor_scalar_add(stat[:, 6:7], stat[:, 5:6], 1e-5)
                        vector.drain()
                        vector.reciprocal(stat[:, 7:8], stat[:, 6:7]).then_inc(sm["sLNa"], 1)
                        vector.wait_ge(sm["sLNb"], TC + tc + 1)
                        vector.drain()
                        vector.tensor_scalar(ysq[:, sl:sl + 1024], ytm[:, sl:sl + 1024],
                                             stat[:, 2:3], stat[:, 8:9],
                                             ALU.subtract, ALU.mult).then_inc(sm["sFN"], 1)
                        vector.drain()

                @blk.sync
                def _(sync):
                    for tc in range(TC):
                        sync.wait_ge(sm["sFN"], tc + 1)
                        sync.dma_start(out=y_d[tc * 128:(tc + 1) * 128, :],
                                       in_=ysq[:, (tc % 2) * 1024:(tc % 2) * 1024 + 1024]
                                       ).then_inc(sm["sty%d" % (tc % 2)], 16)

            @blk.sync
            def _(sync):
                tap(sync, "combT", combT[0:8, 0:T], "sRTC", 1)
                tap(sync, "acc", xf32[:], "sEv", 512)
                if n_stores[0]:
                    sync.wait_ge(sm["st"], 16 * n_stores[0])
                if stage >= 6:
                    sync.wait_ge(sm["sty0"], 16 * TC // 2)
                    sync.wait_ge(sm["sty1"], 16 * TC // 2)

    return nc, tap_d


def _prep_inputs(inputs):
    f32 = np.float32
    base = {
        "wqT": np.ascontiguousarray(np.asarray(inputs["L0_attn_W_Q"], dtype=f32).T),
        "wkT": np.ascontiguousarray(np.asarray(inputs["L0_attn_W_K"], dtype=f32).T),
        "wkpT": np.ascontiguousarray(np.asarray(inputs["L0_attn_W_K_prime"], dtype=f32).T),
        "wvT": np.ascontiguousarray(np.asarray(inputs["L0_attn_W_V"], dtype=f32).T),
        "woT": np.ascontiguousarray(np.asarray(inputs["L0_attn_W_O"], dtype=f32).T),
        "alpha": np.asarray(inputs["L0_attn_alpha"], dtype=f32).reshape(1, 1),
        "wg": np.ascontiguousarray(np.asarray(inputs["L0_ffn_Wg"], dtype=f32)),
        "w1": np.ascontiguousarray(np.asarray(inputs["L0_ffn_W1"], dtype=f32)),
        "w2": np.ascontiguousarray(np.asarray(inputs["L0_ffn_W2"], dtype=f32)),
        "w3": np.ascontiguousarray(np.asarray(inputs["L0_ffn_W3"], dtype=f32)),
    }
    x = np.asarray(inputs["L0_attn_x"], dtype=f32)
    in_maps = []
    for b in range(8):
        m = dict(base)
        m["x"] = np.ascontiguousarray(x[b])
        in_maps.append(m)
    return in_maps


def kernel(**inputs):
    nc = bass.Bass("TRN2", target_bir_lowering=False)
    build(nc)
    in_maps = _prep_inputs(inputs)
    res = run_bass_kernel_spmd(nc, in_maps, core_ids=list(range(8)))
    return np.stack([res.results[b]["y"] for b in range(8)], axis=0)
